# version 12
# baseline (speedup 1.0000x reference)
import math
import numpy as np
import concourse.bass as bass
import concourse.mybir as mybir
from concourse.bass_utils import run_bass_kernel_spmd

F32 = mybir.dt.float32
BF16 = mybir.dt.bfloat16
AF = mybir.ActivationFunctionType
ALU = mybir.AluOpType
AX = mybir.AxisListType

SQRT_HD_INV = 1.0 / math.sqrt(128.0)
EXP_BIAS = -30.0
EPS = 1e-5

FULL_CFG = dict(
    D=2048, H=16, S=8192, INTER=8192, NEXP=8,
    runs=[(0, 14), (14, 26), (26, 36), (36, 44), (44, 50), (50, 56), (56, 60), (60, 64)],
)


def derive(cfg):
    c = dict(cfg)
    c["DC"] = c["D"] // 128
    c["NTILE"] = c["S"] // 128
    c["IT"] = c["INTER"] // 128
    assert c["H"] * 128 == c["D"]
    assert sum(b - a for a, b in c["runs"]) == c["NTILE"]
    c["NTOK_MAX"] = max(b - a for a, b in c["runs"]) * 128
    return c


def plan_for(cfg, core):
    t0, t1 = cfg["runs"][core]
    p = {"t0": t0, "t1": t1, "ntok": (t1 - t0) * 128, "M": t1}
    blocks = []
    t = 0
    while t < t1:
        w = min(4, t1 - t)
        blocks.append((t * 128, w * 128))
        t += w
    p["kvblocks"] = blocks
    qcs = []
    t = t0
    while t < t1:
        w = min(4, t1 - t)
        qcs.append((t, w))
        t += w
    p["qchunks"] = qcs
    passes = []
    npass = (p["ntok"] + 1023) // 1024
    base = p["ntok"] // npass // 128
    rem = p["ntok"] // 128 - base * npass
    off = 0
    for i in range(npass):
        w = (base + (1 if i < rem else 0)) * 128
        passes.append((off, w))
        off += w
    p["ffn_passes"] = passes
    moff = 0
    mask_off = []
    for (ft, wt) in qcs:
        mask_off.append(moff)
        moff += wt * (wt * 128)
    p["mask_off"] = mask_off
    p["mask_cols"] = max(moff, 128)
    return p


def host_masks(cfg, core):
    import ml_dtypes
    p = plan_for(cfg, core)
    maskm = max(plan_for(cfg, c)["mask_cols"] for c in range(8))
    m = np.zeros((128, maskm), dtype=np.float32)
    for (ft, wt), moff in zip(p["qchunks"], p["mask_off"]):
        W = wt * 128
        q_tok0 = ft * 128
        for dk in range(wt):
            kc = ft + dk
            blk = (kc * 128 + np.arange(128)[:, None]) <= (q_tok0 + np.arange(W)[None, :])
            m[:, moff + dk * W: moff + (dk + 1) * W] = blk.astype(np.float32)
    return m.astype(ml_dtypes.bfloat16)


def host_rope_tables(cfg):
    S = cfg["S"]
    freqs = 1.0 / 10000.0 ** (np.arange(0, 128, 2, dtype=np.float32) / 128.0)
    pos = np.arange(S, dtype=np.float32)
    ang = np.outer(freqs.astype(np.float32), pos)
    cosv = np.cos(ang).astype(np.float32)
    sinv = np.sin(ang).astype(np.float32)
    COS = np.concatenate([cosv, cosv], axis=0)
    SIN = np.concatenate([-sinv, sinv], axis=0)
    return np.ascontiguousarray(COS), np.ascontiguousarray(SIN)


def perm_wqk(w):
    D = w.shape[1]
    idx = np.arange(D).reshape(D // 128, 128)
    perm = np.concatenate([idx[:, 0::2], idx[:, 1::2]], axis=1).reshape(-1)
    return np.ascontiguousarray(w[:, perm])


class Ctr:
    def __init__(self, h):
        self.h = h
        self.n = 0

    def inc(self, inst, k):
        inst.then_inc(self.h, k)
        self.n += k
        return self.n


class SerialEng:
    _PASS = {"wait_ge", "partition_id", "dma_start", "Switch", "nop", "engine"}

    def __init__(self, eng, ctr):
        self._e = eng
        self._c = ctr

    def __getattr__(self, name):
        fn = getattr(self._e, name)
        if name in SerialEng._PASS or not callable(fn):
            return fn

        def wrapped(*a, **k):
            self._e.wait_ge(self._c.h, self._c.n)
            inst = fn(*a, **k)
            self._c.n += 1
            inst.then_inc(self._c.h, 1)
            return inst
        return wrapped


def dmaj(ap2d, a, b):
    return ap2d[:, a:b].rearrange("(c p) t -> p c t", p=128)


class Arena:
    def __init__(self, tensor, nbytes):
        self.t = tensor
        self.free = [(0, nbytes)]
        self.toks = {}

    def alloc(self, nbytes, dtype=BF16):
        n = (int(nbytes) + 511) & ~511
        for i, (o, s) in enumerate(self.free):
            if s >= n:
                if s == n:
                    self.free.pop(i)
                else:
                    self.free[i] = (o + n, s - n)
                view = self.t[:, o // 2:(o + n) // 2]
                if dtype is F32:
                    view = view.bitcast(F32)
                self.toks[id(view)] = (o, n)
                return view
        raise MemoryError(f"arena alloc {n}B failed; free={self.free}")

    def release(self, view):
        o, n = self.toks.pop(id(view))
        self.free.append((o, n))
        self.free.sort()
        merged = []
        for seg in self.free:
            if merged and merged[-1][0] + merged[-1][1] == seg[0]:
                merged[-1] = (merged[-1][0], merged[-1][1] + seg[1])
            else:
                merged.append(list(seg) if False else seg)
        self.free = [tuple(x) for x in merged]


def dma(eng, ctr, out, in_, **kw):
    eng.wait_ge(ctr.h, ctr.n)
    return ctr.inc(eng.dma_start(out=out, in_=in_, **kw), 16)


def build_nc(cfg):
    cfg = derive(cfg)
    D, DC, H, S = cfg["D"], cfg["DC"], cfg["H"], cfg["S"]
    INTER, IT, NEXP = cfg["INTER"], cfg["IT"], cfg["NEXP"]
    NTILE, NTOKM = cfg["NTILE"], cfg["NTOK_MAX"]
    MASKM = max(plan_for(cfg, c)["mask_cols"] for c in range(8))

    nc = bass.Bass("TRN2", target_bir_lowering=False, num_devices=8)

    h = {}

    def din(name, shape, dt):
        h[name] = nc.dram_tensor(name, shape, dt, kind="ExternalInput").ap()

    din("xt", [D, S], F32)
    din("wq", [D, D], F32)
    din("wk", [D, D], F32)
    din("wv", [D, D], F32)
    din("wo", [D, D], F32)
    din("upw", [D, INTER], F32)
    din("dnw", [INTER, D], F32)
    din("exw", [D, D], F32)
    din("gtw", [D, NEXP], F32)
    din("gmb", [2, D], F32)
    din("cosk", [128, S], F32)
    din("sink", [128, S], F32)
    din("ident", [128, 128], F32)
    din("pswap", [128, 128], F32)
    din("maskin", [128, MASKM], BF16)
    h["out_t"] = nc.dram_tensor("out_t", [D, NTOKM], F32, kind="ExternalOutput").ap()

    h["kT"] = nc.dram_tensor("kT", [D, S], BF16, kind="Internal").ap()
    h["vhb"] = nc.dram_tensor("vhb", [H, NTILE, 128, 128], BF16, kind="Internal").ap()
    h["xhsp"] = nc.dram_tensor("xhsp", [D, S], BF16, kind="Internal").ap()
    h["x2sp"] = nc.dram_tensor("x2sp", [D, NTOKM], F32, kind="Internal").ap()
    h["xh2sp"] = nc.dram_tensor("xh2sp", [D, NTOKM], BF16, kind="Internal").ap()
    h["t2sp"] = nc.dram_tensor("t2sp", [8, 128], F32, kind="Internal").ap()

    sb = nc.alloc_sbuf_tensor
    h["sqsl"] = sb("sqsl", [128, DC * 512], BF16)
    h["epa"] = sb("epa", [128, 1024], BF16)
    h["epb"] = sb("epb", [128, 1024], BF16)
    h["ktmp"] = sb("ktmp", [128, 512], BF16)
    h["ktmp2"] = sb("ktmp2", [128, 512], BF16)
    h["xtm"] = sb("xtm", [128, 512], BF16)
    h["rtmp"] = sb("rtmp", [128, 512], BF16)
    h["hrel"] = sb("hrel", [128, 512], BF16)
    h["hrel2"] = sb("hrel2", [128, 512], BF16)
    h["cosb"] = sb("cosb", [128, 1024], BF16)
    h["sinb"] = sb("sinb", [128, 1024], BF16)
    h["wstr"] = sb("wstr", [128, 2 * DC * 256], BF16)
    h["gateb"] = sb("gateb", [128, DC * NEXP], BF16)
    h["linv"] = sb("linv", [128, 512], F32)
    h["lgs"] = sb("lgs", [128, 64], F32)
    h["t2col"] = sb("t2col", [128, 8], F32)
    h["t2row"] = sb("t2row", [8, 128], F32)
    h["identb"] = sb("identb", [128, 128], F32)
    h["pswapb"] = sb("pswapb", [128, 128], BF16)
    h["gmbb"] = sb("gmbb", [128, 2 * DC], F32)
    h["st_mu"] = sb("st_mu", [1, 512], F32)
    h["st_rs"] = sb("st_rs", [1, 512], F32)
    h["st_ms"] = sb("st_ms", [1, 512], F32)
    h["st_t1"] = sb("st_t1", [1, 512], F32)
    h["msbc"] = sb("msbc", [128, 512], BF16)
    h["rsbc"] = sb("rsbc", [128, 512], BF16)
    h["ones_c"] = sb("ones_c", [128, 1], BF16)
    h["ones_m"] = sb("ones_m", [128, 128], BF16)
    h["ones_r"] = sb("ones_r", [1, 128], F32)
    h["biasm30"] = sb("biasm30", [128, 1], F32)
    h["epsr"] = sb("epsr", [1, 1], F32)
    rem = nc.sbuf_bytes_remaining
    arena_bytes = (rem - 2048) & ~511
    h["ARENA_BYTES"] = arena_bytes
    h["arena"] = sb("arena", [128, arena_bytes // 2], BF16)

    h["PSALL"] = nc.alloc_psum_tensor("psall", [128, 4096], F32)
    h["PS"] = [h["PSALL"][:, i * 512:(i + 1) * 512] for i in range(8)]

    import contextlib
    ctx = contextlib.ExitStack()
    sem_names = ["sx0", "sx1", "sk0", "sk1", "sm", "sw0", "sw1", "swm",
                 "sg0", "sg1", "sst", "spe", "ssc", "sve"]
    sems = {n: Ctr(ctx.enter_context(nc.semaphore(n))) for n in sem_names}

    T, G, Y = nc.tensor, nc.gpsimd, nc.sync
    V = SerialEng(nc.vector, sems["sve"])
    C = SerialEng(nc.scalar, sems["ssc"])

    V.memset(h["ones_c"][:, :], 1.0)
    V.memset(h["ones_m"][:, :], 1.0)
    V.memset(h["ones_r"][:, :], 1.0)
    V.memset(h["biasm30"][:, :], EXP_BIAS)
    V.memset(h["epsr"][:, :], EPS)
    ve_pre = sems["sve"].n
    dma(Y, sems["sm"], 
        out=h["gmbb"][:, 0:DC],
        in_=h["gmb"][0:1, :].rearrange("o (c p) -> p (o c)", p=128),
        allow_slow_non_contiguous=True)
    dma(Y, sems["sm"], 
        out=h["gmbb"][:, DC:2 * DC],
        in_=h["gmb"][1:2, :].rearrange("o (c p) -> p (o c)", p=128),
        allow_slow_non_contiguous=True)
    dma(Y, sems["sm"], out=h["identb"][:, :], in_=h["ident"][:, :])
    dma(G, sems["swm"], out=h["pswapb"][:, :], in_=h["pswap"][:, :])
    sm_pre = sems["sm"].n
    h["ve_pre"], h["sm_pre"] = ve_pre, sm_pre

    engines = [T, nc.vector, nc.scalar, G, Y]
    pre_state = {n: s.n for n, s in sems.items()}
    for core in nc.Switch(engines=engines,
                          index=[e.partition_id() for e in engines], n=8):
        for n, s in sems.items():
            s.n = pre_state[n]
        emit_body(nc, cfg, plan_for(cfg, core), sems, h)

    ctx.close()
    return nc


def emit_body(nc, cfg, p, sems, h):
    D, DC, H, S = cfg["D"], cfg["DC"], cfg["H"], cfg["S"]
    INTER, IT, NEXP = cfg["INTER"], cfg["IT"], cfg["NEXP"]
    T, G, Y = nc.tensor, nc.gpsimd, nc.sync
    V = SerialEng(nc.vector, sems["sve"])
    C = SerialEng(nc.scalar, sems["ssc"])
    PS = h["PS"]
    sx = [sems["sx0"], sems["sx1"]]
    sg = [sems["sg0"], sems["sg1"]]
    sk = [sems["sk0"], sems["sk1"]]
    sw = [sems["sw0"], sems["sw1"]]
    sm, swm, sst = sems["sm"], sems["swm"], sems["sst"]
    spe, ssc, sve = sems["spe"], sems["ssc"], sems["sve"]

    xt, kT, vhb, xhsp = h["xt"], h["kT"], h["vhb"], h["xhsp"]
    x2sp, xh2sp, t2sp, out_t = h["x2sp"], h["xh2sp"], h["t2sp"], h["out_t"]

    def stats(src_ap, TW, stride):
        if stride == TW:
            nh = (DC * TW + 4095) // 4096
            for k in range(nh):
                lo = k * 4096
                hi = min(DC * TW, lo + 4096)
                C.activation(out=h["sqsl"][:, lo:hi],
                             in_=src_ap[:, lo:hi], func=AF.Square)
        else:
            for c in range(DC):
                C.activation(out=h["sqsl"][:, c * TW:(c + 1) * TW],
                             in_=src_ap[:, c * stride: c * stride + TW],
                             func=AF.Square)
        scc = ssc.n
        for c in range(DC):
            mm = T.matmul(PS[6][0:1, 0:TW], h["ones_c"][:, 0:1],
                          src_ap[:, c * stride: c * stride + TW],
                          start=(c == 0), stop=(c == DC - 1))
        spe.inc(mm, 1)
        T.wait_ge(ssc.h, scc)
        for c in range(DC):
            mm = T.matmul(PS[7][0:1, 0:TW], h["ones_c"][:, 0:1],
                          h["sqsl"][:, c * TW:(c + 1) * TW],
                          start=(c == 0), stop=(c == DC - 1))
        last_sq = spe.inc(mm, 1)
        V.wait_ge(spe.h, last_sq)
        V.tensor_scalar_mul(h["st_mu"][:, 0:TW], PS[6][0:1, 0:TW], 1.0 / D)
        V.tensor_mul(h["st_ms"][:, 0:TW], h["st_mu"][:, 0:TW], h["st_mu"][:, 0:TW])
        V.scalar_tensor_tensor(h["st_t1"][:, 0:TW], PS[7][0:1, 0:TW], 1.0 / D,
                               h["st_ms"][:, 0:TW], ALU.mult, ALU.subtract)
        vv = sve.n
        C.wait_ge(sve.h, vv)
        C.activation(out=h["st_t1"][:, 0:TW], in_=h["st_t1"][:, 0:TW],
                     func=AF.Ln, bias=h["epsr"][:, :])
        C.activation(out=h["st_rs"][:, 0:TW], in_=h["st_t1"][:, 0:TW],
                     func=AF.Exp, scale=-0.5)
        cc = ssc.n
        V.wait_ge(ssc.h, cc)
        V.tensor_mul(h["st_ms"][:, 0:TW], h["st_mu"][:, 0:TW], h["st_rs"][:, 0:TW])
        vv = sve.n
        T.wait_ge(sve.h, vv)
        T.matmul(PS[6][:, 0:TW], h["ones_r"][:, :], h["st_rs"][0:1, 0:TW],
                 start=True, stop=True)
        pe = spe.inc(T.matmul(PS[7][:, 0:TW], h["ones_r"][:, :],
                              h["st_ms"][0:1, 0:TW], start=True, stop=True), 1)
        V.wait_ge(spe.h, pe)
        V.tensor_copy(h["rsbc"][:, 0:TW], PS[6][:, 0:TW])
        V.tensor_copy(h["msbc"][:, 0:TW], PS[7][:, 0:TW])
        done = sve.n
        T.wait_ge(sve.h, done)
        return done

    for e in (T, V, C, G, Y):
        e.wait_ge(sems["sve"].h, h["ve_pre"])
        e.wait_ge(sems["sm"].h, h["sm_pre"])
        e.wait_ge(sems["swm"].h, 16)

    A = Arena(h["arena"], h["ARENA_BYTES"])
    KBW = 512
    wbig = A.alloc(DC * D * 2)
    xbfs = [A.alloc(DC * KBW * 2), A.alloc(DC * KBW * 2)]
    kvsl = A.alloc(DC * KBW * 2)
    xhscs = [A.alloc(DC * KBW * 2), A.alloc(DC * KBW * 2)]
    nblk = len(p["kvblocks"])

    w0 = dma(G, sw[0], 
        out=wbig[:, 0:DC * D].rearrange("p (c o) -> p c o", o=D),
        in_=h["wk"].rearrange("(c p) o -> p c o", p=128))
    ve_xdone = {}
    st_kv = {}
    st_xh = {}
    pe_k_done = {}
    stat_mm_of = {}
    ve_xh_of = {}
    lw_cs = {}
    kevac = {}
    ve_rope = 0
    ve_rope_b = {}
    KT = [h["ktmp"], h["ktmp2"]]
    nblk = len(p["kvblocks"])

    def kv_emit_A1(b):
        toff, BW = p["kvblocks"][b]
        buf = xbfs[b % 2]
        if b >= 2:
            G.wait_ge(spe.h, stat_mm_of[b - 2])
            G.wait_ge(sve.h, ve_xh_of[b - 2])
        lx = dma(G, sg[b % 2], out=buf[:, 0:DC * BW],
                 in_=dmaj(xt, toff, toff + BW))
        cpar = (b % 2) * 512
        if b >= 2:
            G.wait_ge(sve.h, ve_rope_of.get(b - 2, 0))
        lw_cs[b] = dma(G, swm, out=h["cosb"][:, cpar:cpar + BW],
                       in_=h["cosk"][:, toff:toff + BW])
        lw_cs[b] = dma(G, swm, out=h["sinb"][:, cpar:cpar + BW],
                       in_=h["sink"][:, toff:toff + BW])
        ve_xdone[b] = lx

    def kv_emit_A2(b):
        toff, BW = p["kvblocks"][b]
        xbf = xbfs[b % 2]
        C.wait_ge(sg[b % 2].h, ve_xdone[b])
        T.wait_ge(sg[b % 2].h, ve_xdone[b])
        vstat = stats(xbf, BW, BW)
        stat_mm_of[b] = spe.n
        V.wait_ge(sve.h, vstat)
        if b >= 2:
            V.wait_ge(spe.h, pe_k_done[b - 2])
            V.wait_ge(sst.h, st_xh[b - 2])
        xh = xhscs[b % 2]
        for c in range(DC):
            V.tensor_mul(h["xtm"][:, 0:BW], xbf[:, c * BW:(c + 1) * BW],
                         h["rsbc"][:, 0:BW])
            V.tensor_sub(xh[:, c * BW:(c + 1) * BW],
                         h["xtm"][:, 0:BW], h["msbc"][:, 0:BW])
        ve_xh_of[b] = sve.n
        Y.wait_ge(sve.h, ve_xh_of[b])
        st_xh[b] = dma(Y, sst, out=dmaj(xhsp, toff, toff + BW),
                       in_=xh[:, 0:DC * BW])

    def kv_emit_B(b):
        nonlocal ve_rope
        toff, BW = p["kvblocks"][b]
        cpar = (b % 2) * 512
        xh = xhscs[b % 2]
        evac_of, pe_swap_of = {}, {}
        T.wait_ge(sve.h, ve_xh_of[b])
        if b == 0:
            T.wait_ge(sw[0].h, w0)
            V.wait_ge(sw[0].h, w0)

        def kv_swap_rope(o):
            nonlocal ve_rope
            swb = PS[2 + o % 2]
            T.wait_ge(ssc.h, evac_of[o])
            if ve_rope_b.get(o % 2):
                T.wait_ge(sve.h, ve_rope_b[o % 2])
            pe_swap_of[o] = spe.inc(
                T.matmul(swb[:, 0:BW], h["pswapb"][:, :],
                         KT[o % 2][:, 0:BW], start=True, stop=True), 1)
            V.wait_ge(spe.h, pe_swap_of[o])
            V.wait_ge(swm.h, lw_cs[b])
            if o == 0 and b >= 1:
                V.wait_ge(sst.h, st_kv[b - 1])
            ksl = kvsl[:, o * BW:(o + 1) * BW]
            V.tensor_mul(ksl, KT[o % 2][:, 0:BW], h["cosb"][:, cpar:cpar + BW])
            V.tensor_mul(h["rtmp"][:, 0:BW], swb[:, 0:BW],
                         h["sinb"][:, cpar:cpar + BW])
            V.tensor_add(ksl, ksl, h["rtmp"][:, 0:BW])
            ve_rope = sve.n
            ve_rope_b[o % 2] = ve_rope

        for o in range(H):
            T.wait_ge(ssc.h, kevac.get(o % 2, 0))
            for c in range(DC):
                mm = T.matmul(PS[o % 2][:, 0:BW],
                              wbig[:, c * D + o * 128: c * D + (o + 1) * 128],
                              xh[:, c * BW:(c + 1) * BW],
                              start=(c == 0), stop=(c == DC - 1))
            pe = spe.inc(mm, 1)
            C.wait_ge(spe.h, pe)
            if o >= 2:
                C.wait_ge(spe.h, pe_swap_of[o - 2])
                C.wait_ge(sve.h, ve_rope_b.get(o % 2, 0))
            C.copy(out=KT[o % 2][:, 0:BW], in_=PS[o % 2][:, 0:BW])
            evac_of[o] = ssc.n
            kevac[o % 2] = evac_of[o]
            if o >= 1:
                kv_swap_rope(o - 1)
        kv_swap_rope(H - 1)
        ve_rope_of[b] = ve_rope
        pe_k_done[b] = spe.n
        Y.wait_ge(sve.h, ve_rope)
        st_kv[b] = dma(Y, sst, out=dmaj(kT, toff, toff + BW),
                       in_=kvsl[:, 0:DC * BW])

    ve_rope_of = {}
    for b in range(nblk):
        kv_emit_A1(b)
        if b >= 1:
            kv_emit_B(b - 1)
        kv_emit_A2(b)
    kv_emit_B(nblk - 1)

    G.wait_ge(spe.h, spe.n)
    Y.wait_ge(spe.h, spe.n)
    w0 = dma(G, sw[0], 
        out=wbig[:, 0:DC * D].rearrange("p (c o) -> p c o", o=D),
        in_=h["wv"].rearrange("(c p) o -> p c o", p=128))
    xhbufs = [xbfs[0], xbfs[1]]
    st_v_last = sst.n
    pe_v_done = {}
    vevac_b = {}
    st_v_2ago = sst.n
    for b, (toff, BW) in enumerate(p["kvblocks"]):
        buf = xhbufs[b % 2]
        if b >= 2:
            Y.wait_ge(spe.h, pe_v_done[b - 2])
        Y.wait_ge(sst.h, st_xh[b])
        lx = dma(Y, sx[b % 2], out=buf[:, 0:DC * BW],
                 in_=dmaj(xhsp, toff, toff + BW))
        T.wait_ge(sx[b % 2].h, lx)
        if b == 0:
            T.wait_ge(sw[0].h, w0)
        ntt = BW // 128
        noc = (D + 511) // 512
        half_cols = kvsl.shape[1] // 2
        for tt in range(ntt):
            gt = (toff // 128) + tt
            par2 = gt % 2
            vsl = kvsl[:, par2 * half_cols: par2 * half_cols + D] \
                if half_cols >= D else kvsl[:, 0:D]
            bank0 = par2 * 4 if noc <= 4 else 0
            T.wait_ge(ssc.h, vevac_b.get(par2, 0))
            for c in range(DC):
                for oc in range(noc):
                    OW = min(512, D - oc * 512)
                    mm = T.matmul(PS[bank0 + oc][:, 0:OW],
                                  buf[:, c * BW + tt * 128: c * BW + (tt + 1) * 128],
                                  wbig[:, c * D + oc * 512: c * D + oc * 512 + OW],
                                  start=(c == 0), stop=(c == DC - 1))
            pe = spe.inc(mm, 1)
            C.wait_ge(spe.h, pe)
            C.wait_ge(sst.h, st_v_2ago if half_cols >= D else st_v_last)
            st_v_2ago = st_v_last
            for oc in range(noc):
                OW = min(512, D - oc * 512)
                C.copy(out=vsl[:, oc * 512: oc * 512 + OW],
                       in_=PS[bank0 + oc][:, 0:OW])
                scc = ssc.n
            vevac_b[par2] = scc
            Y.wait_ge(ssc.h, scc)
            st_v_last = dma(Y, sst, 
                out=vhb[:, gt:gt + 1, :, :].rearrange("a o p c -> p (o a) c"),
                in_=vsl[:, 0:D].rearrange("p (a c) -> p a c", c=128))
        pe_v_done[b] = spe.n

    A.release(kvsl)
    A.release(wbig)
    A.release(xbfs[0])
    A.release(xbfs[1])
    A.release(xhscs[0])
    A.release(xhscs[1])
    ntok, t0 = p["ntok"], p["t0"]
    qslab = A.alloc(DC * ntok * 2)
    xhoq = A.alloc(DC * ntok * 2)
    cosq_v = A.alloc(ntok * 2)
    sinq_v = A.alloc(ntok * 2)
    Y.wait_ge(spe.h, spe.n)
    Y.wait_ge(sst.h, sst.n)
    G.wait_ge(sst.h, sst.n)
    G.wait_ge(spe.h, spe.n)
    G.wait_ge(sve.h, sve.n)
    lx = dma(Y, sm, out=xhoq[:, 0:DC * ntok],
                            in_=dmaj(xhsp, t0 * 128, t0 * 128 + ntok))
    lw = dma(G, swm, out=cosq_v[:, 0:ntok],
                             in_=h["cosk"][:, t0 * 128: t0 * 128 + ntok])
    lw = dma(G, swm, out=sinq_v[:, 0:ntok],
                             in_=h["sink"][:, t0 * 128: t0 * 128 + ntok])
    T.wait_ge(sm.h, lx)
    V.wait_ge(swm.h, lw)
    nqc = (ntok + 511) // 512
    pe_q = {}
    qevac = {}
    KT = [h["ktmp"], h["ktmp2"]]
    q_evac_of, q_swap_of, q_rope_b = {}, {}, {}
    lw_qo = {}

    def load_wq(op):
        half = op % 2
        if op >= 2:
            G.wait_ge(spe.h, pe_q[2 * op - 4 + 1])
        lw_qo[op] = dma(G, sw[half],
            out=h["wstr"][:, half * DC * 256:(half + 1) * DC * 256]
            .rearrange("p (c o) -> p c o", o=256),
            in_=h["wq"][:, op * 256:(op + 1) * 256]
            .rearrange("(c p) o -> p c o", p=128))

    def q_swap_rope(idx):
        o, qc = divmod(idx, nqc)
        W = min(512, ntok - qc * 512)
        kb = idx % 2
        swb = PS[2 + kb]
        T.wait_ge(ssc.h, q_evac_of[idx])
        if q_rope_b.get(kb):
            T.wait_ge(sve.h, q_rope_b[kb])
        q_swap_of[idx] = spe.inc(
            T.matmul(swb[:, 0:W], h["pswapb"][:, :],
                     KT[kb][:, 0:W], start=True, stop=True), 1)
        V.wait_ge(spe.h, q_swap_of[idx])
        qsl = qslab[:, o * ntok + qc * 512: o * ntok + qc * 512 + W]
        cq = cosq_v[:, qc * 512: qc * 512 + W]
        sq_ = sinq_v[:, qc * 512: qc * 512 + W]
        V.tensor_mul(qsl, KT[kb][:, 0:W], cq)
        V.tensor_mul(h["rtmp"][:, 0:W], swb[:, 0:W], sq_)
        V.tensor_add(qsl, qsl, h["rtmp"][:, 0:W])
        q_rope_b[kb] = sve.n

    load_wq(0)
    if H // 2 > 1:
        load_wq(1)
    for o in range(H):
        op_ = o // 2
        half = op_ % 2
        wqv = h["wstr"][:, half * DC * 256:(half + 1) * DC * 256]
        woff = (o % 2) * 128
        T.wait_ge(sw[half].h, lw_qo[op_])
        for qc in range(nqc):
            W = min(512, ntok - qc * 512)
            idx = o * nqc + qc
            qb = idx % 2
            T.wait_ge(ssc.h, qevac.get(qb, 0))
            for c in range(DC):
                mm = T.matmul(PS[qb][:, 0:W],
                              wqv[:, c * 256 + woff: c * 256 + woff + 128],
                              xhoq[:, c * ntok + qc * 512: c * ntok + qc * 512 + W],
                              start=(c == 0), stop=(c == DC - 1))
            pe_q[o] = spe.inc(mm, 1)
            C.wait_ge(spe.h, pe_q[o])
            if idx >= 2:
                C.wait_ge(spe.h, q_swap_of[idx - 2])
                C.wait_ge(sve.h, q_rope_b.get(qb, 0))
            C.copy(out=KT[qb][:, 0:W], in_=PS[qb][:, 0:W])
            q_evac_of[idx] = ssc.n
            qevac[qb] = q_evac_of[idx]
            if idx >= 1:
                q_swap_rope(idx - 1)
        if o % 2 == 1 and o // 2 + 2 < H // 2:
            load_wq(o // 2 + 2)
    q_swap_rope(H * nqc - 1)
    ve_qrope = sve.n

    A.release(xhoq)
    A.release(cosq_v)
    A.release(sinq_v)
    M = p["M"]
    ext = M * 128
    ctx = A.alloc(DC * ntok * 2)
    khbufs = [A.alloc(ext * 2), A.alloc(ext * 2)]
    vhbufs = [A.alloc(ext * 2), A.alloc(ext * 2)]
    max_mask = max(wt_ * wt_ * 128 for (_, wt_) in p["qchunks"])
    maskv = A.alloc(max(max_mask * 2, 512))
    ve_mask = 0
    Y.wait_ge(spe.h, spe.n)
    Y.wait_ge(sve.h, sve.n)
    Y.wait_ge(ssc.h, ssc.n)
    T.wait_ge(sve.h, ve_qrope)
    Y.wait_ge(sst.h, sst.n)
    pe_head_done = {}
    PSALL = h["PSALL"]
    SApair = [PSALL[:, 0:1024], PSALL[:, 2048:3072]]
    EPAIR = [h["epa"], h["epb"]]
    LB = [PS[2], PS[6]]
    CB = [PS[3], PS[7]]
    sc_exp = {0: 0, 1: 0}
    pe_cons = {0: 0, 1: 0}
    ve_ctx_b = {0: 0, 1: 0}
    qcidx = 0
    ve_mask = 0
    for hh in range(H):
        par = hh % 2
        if hh >= 2:
            Y.wait_ge(spe.h, pe_head_done[hh - 2])
        lk = dma(Y, sk[par], out=khbufs[par][:, 0:ext],
                 in_=kT[hh * 128:(hh + 1) * 128, 0:ext])
        lk = dma(Y, sk[par], 
            out=vhbufs[par][:, 0:ext].rearrange("p (m c) -> p m c", c=128),
            in_=vhb[hh:hh + 1, 0:M, :, :].rearrange("o m p c -> p (o m) c"))
        T.wait_ge(sk[par].h, lk)
        for (qi, (ft, wt)) in enumerate(p["qchunks"]):
            W = wt * 128
            qoff = (ft - p["t0"]) * 128
            E = ft + wt
            moff = p["mask_off"][qi]
            mcols = wt * W
            if ve_mask:
                Y.wait_ge(sve.h, ve_mask)
            lmsk = dma(Y, sm, out=maskv[:, 0:mcols],
                       in_=h["maskin"][:, moff:moff + mcols])
            gates = {}

            def emit_pair(kp):
                nonlocal ve_mask
                pr = (kp // 2) % 2
                sa = SApair[pr]
                ebuf = EPAIR[pr]
                n2 = 2 if kp + 1 < E else 1
                T.wait_ge(ssc.h, sc_exp[pr])
                for j in range(n2):
                    mm = T.matmul(sa[:, j * 512: j * 512 + W],
                                  khbufs[par][:, (kp + j) * 128:(kp + j + 1) * 128],
                                  qslab[:, hh * ntok + qoff: hh * ntok + qoff + W],
                                  start=True, stop=True)
                pe_s = spe.inc(mm, 1)
                C.wait_ge(spe.h, pe_s)
                C.wait_ge(spe.h, pe_cons[pr])
                if n2 == 2 and W == 512:
                    C.activation(out=ebuf[:, 0:1024], in_=sa[:, 0:1024],
                                 func=AF.Exp, scale=SQRT_HD_INV,
                                 bias=h["biasm30"][:, :])
                else:
                    for j in range(n2):
                        C.activation(out=ebuf[:, j * 512: j * 512 + W],
                                     in_=sa[:, j * 512: j * 512 + W],
                                     func=AF.Exp, scale=SQRT_HD_INV,
                                     bias=h["biasm30"][:, :])
                e = ssc.n
                sc_exp[pr] = e
                for j in range(n2):
                    kc = kp + j
                    dk = kc - ft
                    if 0 <= dk < wt:
                        V.wait_ge(ssc.h, e)
                        V.wait_ge(sm.h, lmsk)
                        V.tensor_mul(ebuf[:, j * 512: j * 512 + W],
                                     ebuf[:, j * 512: j * 512 + W],
                                     maskv[:, dk * W: (dk + 1) * W])
                        ve_mask = sve.n
                        gates[kc] = ("ve", ve_mask)
                    else:
                        gates[kc] = ("sc", e)

            def emit_sumav_pair(kp):
                pr = (kp // 2) % 2
                ebuf = EPAIR[pr]
                n2 = 2 if kp + 1 < E else 1
                for j in range(n2):
                    kc = kp + j
                    kind, val = gates.pop(kc)
                    if kind == "ve":
                        T.wait_ge(sve.h, val)
                    else:
                        T.wait_ge(ssc.h, val)
                    if kc == 0:
                        T.wait_ge(sve.h, ve_ctx_b[qcidx % 2])
                    T.matmul(LB[qcidx % 2][:, 0:W], h["ones_m"][:, :],
                             ebuf[:, j * 512: j * 512 + W],
                             start=(kc == 0), stop=(kc == E - 1))
                    mm = T.matmul(CB[qcidx % 2][:, 0:W],
                                  vhbufs[par][:, kc * 128:(kc + 1) * 128],
                                  ebuf[:, j * 512: j * 512 + W],
                                  start=(kc == 0), stop=(kc == E - 1))
                pe_cons[pr] = spe.inc(mm, 1)

            pairs = list(range(0, E, 2))
            for pi, kp in enumerate(pairs):
                emit_pair(kp)
                if pi >= 1:
                    emit_sumav_pair(pairs[pi - 1])
            emit_sumav_pair(pairs[-1])
            pe_fin = spe.n
            V.wait_ge(spe.h, pe_fin)
            V.reciprocal(h["linv"][:, 0:W], LB[qcidx % 2][:, 0:W])
            csl = ctx[:, hh * ntok + qoff: hh * ntok + qoff + W]
            V.tensor_mul(csl, CB[qcidx % 2][:, 0:W], h["linv"][:, 0:W])
            ve_ctx_b[qcidx % 2] = sve.n
            qcidx += 1
        pe_head_done[hh] = spe.n

    A.release(qslab)
    for bb_ in khbufs + vhbufs:
        A.release(bb_)
    A.release(maskv)
    xfa2 = A.alloc(DC * 512 * 4, F32)
    xbf2 = A.alloc(DC * 512 * 2)
    xbf2b = A.alloc(DC * 512 * 2)
    Y.wait_ge(spe.h, spe.n)
    Y.wait_ge(sve.h, sve.n)
    Y.wait_ge(ssc.h, ssc.n)
    T.wait_ge(sve.h, sve.n)
    ntc = (ntok + 511) // 512
    pe_wo = spe.n
    woevac = {}
    a_done = {}
    st_spill_of = {}
    st_wo_prev = sst.n
    lx_of = {}
    xbf2s = [xbf2, xbf2b]

    lx_of[0] = dma(Y, sx[0], out=xfa2[:, 0:DC * min(512, ntok)],
                   in_=dmaj(xt, t0 * 128, t0 * 128 + min(512, ntok)))

    def wo_emit_B(tc):
        nonlocal pe_wo
        TW = min(512, ntok - tc * 512)
        toff = tc * 512
        xb2 = xbf2s[tc % 2]
        lw_of, pe_wo_of = {}, {}

        def load_wo(op):
            half = op % 2
            if op >= 2:
                G.wait_ge(spe.h, pe_wo_of[2 * op - 3])
            elif pe_wo:
                G.wait_ge(spe.h, pe_wo)
            lw_of[op] = dma(G, sw[half],
                out=h["wstr"][:, half * DC * 256:(half + 1) * DC * 256]
                .rearrange("p (c o) -> p c o", o=256),
                in_=h["wo"][:, op * 256:(op + 1) * 256]
                .rearrange("(c p) o -> p c o", p=128))

        load_wo(0)
        if H // 2 > 1:
            load_wo(1)
        for o in range(H):
            op_ = o // 2
            half = op_ % 2
            wov = h["wstr"][:, half * DC * 256:(half + 1) * DC * 256]
            woff = (o % 2) * 128
            T.wait_ge(sw[half].h, lw_of[op_])
            T.wait_ge(ssc.h, woevac.get(o % 2, 0))
            for c in range(DC):
                mm = T.matmul(PS[o % 2][:, 0:TW],
                              wov[:, c * 256 + woff: c * 256 + woff + 128],
                              ctx[:, c * ntok + toff: c * ntok + toff + TW],
                              start=(c == 0), stop=(c == DC - 1))
            pe_wo = spe.inc(mm, 1)
            pe_wo_of[o] = pe_wo
            C.wait_ge(spe.h, pe_wo)
            if tc >= 2 and o == 0:
                C.wait_ge(sve.h, a_done[tc - 2])
                C.wait_ge(sst.h, st_spill_of[tc - 2])
            C.copy(out=xb2[:, o * TW:(o + 1) * TW],
                   in_=PS[o % 2][:, 0:TW])
            woevac[o % 2] = ssc.n
            if o % 2 == 1 and o // 2 + 2 < H // 2:
                load_wo(o // 2 + 2)

    def wo_emit_A(tc):
        nonlocal st_wo_prev
        TW = min(512, ntok - tc * 512)
        toff = tc * 512
        xb2 = xbf2s[tc % 2]
        T.wait_ge(ssc.h, woevac[0])
        T.wait_ge(ssc.h, max(woevac.values()))
        vstat = stats(xb2, TW, TW)
        V.wait_ge(sve.h, vstat)
        V.wait_ge(sx[0].h, lx_of[tc])
        V.wait_ge(sst.h, st_wo_prev)
        for c in range(DC):
            V.tensor_mul(h["ktmp"][:, 0:TW], xb2[:, c * TW:(c + 1) * TW],
                         h["rsbc"][:, 0:TW])
            V.tensor_sub(h["ktmp"][:, 0:TW], h["ktmp"][:, 0:TW], h["msbc"][:, 0:TW])
            V.tensor_tensor(xfa2[:, c * TW:(c + 1) * TW],
                            xfa2[:, c * TW:(c + 1) * TW],
                            h["ktmp"][:, 0:TW], ALU.add)
        ve_x2 = sve.n
        Y.wait_ge(sve.h, ve_x2)
        dma(Y, sst, out=dmaj(x2sp, toff, toff + TW), in_=xfa2[:, 0:DC * TW])
        V.tensor_copy(xb2[:, 0:DC * TW], xfa2[:, 0:DC * TW])
        vv = sve.n
        C.wait_ge(sve.h, vv)
        T.wait_ge(sve.h, vv)
        vstat = stats(xb2, TW, TW)
        V.wait_ge(sve.h, vstat)
        for c in range(DC):
            V.tensor_mul(h["ktmp"][:, 0:TW], xb2[:, c * TW:(c + 1) * TW],
                         h["rsbc"][:, 0:TW])
            V.tensor_sub(xb2[:, c * TW:(c + 1) * TW],
                         h["ktmp"][:, 0:TW], h["msbc"][:, 0:TW])
        a_done[tc] = sve.n
        Y.wait_ge(sve.h, a_done[tc])
        dma(Y, sst, out=dmaj(xh2sp, toff, toff + TW), in_=xb2[:, 0:DC * TW])
        st_wo_prev = sst.n
        st_spill_of[tc] = sst.n
        if tc + 1 < ntc:
            TW2 = min(512, ntok - (tc + 1) * 512)
            Y.wait_ge(sst.h, st_wo_prev)
            lx_of[tc + 1] = dma(Y, sx[0],
                out=xfa2[:, 0:DC * TW2],
                in_=dmaj(xt, t0 * 128 + (tc + 1) * 512,
                         t0 * 128 + (tc + 1) * 512 + TW2))

    for tc in range(ntc):
        wo_emit_B(tc)
        if tc >= 1:
            wo_emit_A(tc - 1)
    wo_emit_A(ntc - 1)

    A.release(xfa2)
    A.release(xbf2)
    A.release(xbf2b)
    A.release(ctx)
    lwg = dma(G, swm, 
        out=h["gateb"][:, 0:DC * NEXP].rearrange("p (c e) -> p c e", e=NEXP),
        in_=h["gtw"].rearrange("(c p) e -> p c e", p=128))
    n_grp = (IT + 7) // 8
    for (poff, PTOK) in p["ffn_passes"]:
        cs = A.alloc(DC * PTOK * 2)
        sslv = A.alloc(DC * PTOK * 2)
        t2sb_c = A.alloc(4096, F32)
        t2bcb_c = A.alloc(2048)
        Y.wait_ge(sst.h, sst.n)
        Y.wait_ge(spe.h, spe.n)
        Y.wait_ge(sve.h, sve.n)
        Y.wait_ge(ssc.h, ssc.n)
        lx = dma(Y, sm, out=cs[:, 0:DC * PTOK],
                 in_=dmaj(xh2sp, poff, poff + PTOK))
        T.wait_ge(sm.h, lx)
        V.wait_ge(sm.h, lx)
        T.wait_ge(swm.h, lwg)
        V.memset(h["t2col"][:, :], 0.0)
        ntt = PTOK // 128
        for tt in range(ntt):
            for c in range(DC):
                mm = T.matmul(PS[4][:, 0:NEXP],
                              cs[:, c * PTOK + tt * 128: c * PTOK + (tt + 1) * 128],
                              h["gateb"][:, c * NEXP:(c + 1) * NEXP],
                              start=(c == 0), stop=(c == DC - 1))
            pe = spe.inc(mm, 1)
            V.wait_ge(spe.h, pe)
            V.tensor_copy(h["lgs"][:, 0:NEXP], PS[4][:, 0:NEXP])
            off = NEXP
            for k in range(1, NEXP):
                V.tensor_add(h["lgs"][:, off:off + NEXP - k],
                             h["lgs"][:, 0:NEXP - k], h["lgs"][:, k:NEXP])
                off += NEXP - k
            V.reduce_max(h["t2col"][:, tt:tt + 1],
                                      h["lgs"][:, NEXP:off], axis=AX.X)
            vv = sve.n
            T.wait_ge(sve.h, vv)
        mmt = T.transpose(PS[5][0:8, 0:128], h["t2col"][:, 0:8], h["identb"][:, :])
        pe = spe.inc(mmt, 1)
        V.wait_ge(spe.h, pe)
        V.tensor_copy(h["t2row"][:, :], PS[5][0:8, 0:128])
        vv = sve.n
        Y.wait_ge(sve.h, vv)
        ls = dma(Y, sst, out=t2sp, in_=h["t2row"][:, :])
        Y.wait_ge(sst.h, ls)
        lx2 = dma(Y, sm, out=t2sb_c[0:1, 0:PTOK],
                                 in_=t2sp.rearrange("a b -> (a b)")[0:PTOK])
        T.wait_ge(sm.h, lx2)
        for tcc in range((PTOK + 511) // 512):
            Wc = min(512, PTOK - tcc * 512)
            mm = T.matmul(PS[4][:, 0:Wc], h["ones_r"][:, :],
                          t2sb_c[0:1, tcc * 512: tcc * 512 + Wc],
                          start=True, stop=True)
            pe = spe.inc(mm, 1)
            V.wait_ge(spe.h, pe)
            V.tensor_copy(t2bcb_c[:, tcc * 512: tcc * 512 + Wc],
                                       PS[4][:, 0:Wc])
            vv = sve.n
            T.wait_ge(sve.h, vv)
        pe_moe = spe.n
        lw_mo, pe_mo_of = {}, {}

        def load_moe(op):
            half = op % 2
            if op >= 2:
                G.wait_ge(spe.h, pe_mo_of[2 * op - 3])
            else:
                G.wait_ge(spe.h, pe_moe)
            lw_mo[op] = dma(G, sw[half],
                out=h["wstr"][:, half * DC * 256:(half + 1) * DC * 256]
                .rearrange("p (c o) -> p c o", o=256),
                in_=h["exw"][:, op * 256:(op + 1) * 256]
                .rearrange("(c p) o -> p c o", p=128))

        load_moe(0)
        if H // 2 > 1:
            load_moe(1)
        for o in range(H):
            op_ = o // 2
            half = op_ % 2
            exv = h["wstr"][:, half * DC * 256:(half + 1) * DC * 256]
            eoff = (o % 2) * 128
            T.wait_ge(sw[half].h, lw_mo[op_])
            for tcc in range((PTOK + 511) // 512):
                Wc = min(512, PTOK - tcc * 512)
                for c in range(DC):
                    mm = T.matmul(PS[(o + tcc) % 2][:, 0:Wc],
                                  exv[:, c * 256 + eoff: c * 256 + eoff + 128],
                                  cs[:, c * PTOK + tcc * 512: c * PTOK + tcc * 512 + Wc],
                                  start=(c == 0), stop=(c == DC - 1))
                pe_moe = spe.inc(mm, 1)
                pe_mo_of[o] = pe_moe
                V.wait_ge(spe.h, pe_moe)
                V.tensor_mul(
                    sslv[:, o * PTOK + tcc * 512: o * PTOK + tcc * 512 + Wc],
                    PS[(o + tcc) % 2][:, 0:Wc],
                    t2bcb_c[:, tcc * 512: tcc * 512 + Wc])
                vv = sve.n
                T.wait_ge(sve.h, vv)
            if o % 2 == 1 and o // 2 + 2 < H // 2:
                load_moe(o // 2 + 2)
        GSZ = min(4, IT)
        n_grp4 = (IT + GSZ - 1) // GSZ
        upgs = [A.alloc(DC * GSZ * 128 * 2), A.alloc(DC * GSZ * 128 * 2)]
        dngs = [A.alloc(GSZ * D * 2), A.alloc(GSZ * D * 2)]
        h2slv = A.alloc(GSZ * 512 * 2)
        G.wait_ge(spe.h, spe.n)
        G.wait_ge(sve.h, sve.n)
        pe_grp_done = {}
        lwus, lwds = {}, {}
        relu_done = {}
        sq_done = {}
        evac_done = {}
        pe_dn_last = spe.n
        ve_h2 = 0

        def load_grp(g):
            gpar = g % 2
            i0_ = g * GSZ
            gi_ = min(GSZ, IT - i0_)
            if g >= 2:
                G.wait_ge(spe.h, pe_grp_done[g - 2])
            lwus[g] = dma(G, sw[0],
                out=upgs[gpar][:, 0:DC * gi_ * 128]
                .rearrange("p (c i) -> p c i", i=gi_ * 128),
                in_=h["upw"][:, i0_ * 128:(i0_ + gi_) * 128]
                .rearrange("(c p) i -> p c i", p=128))
            lwds[g] = dma(G, sw[1],
                out=dngs[gpar][:, 0:gi_ * D].rearrange("p (a o) -> p a o", o=D),
                in_=h["dnw"][i0_ * 128:(i0_ + gi_) * 128, :]
                .rearrange("(a p) o -> p a o", p=128))

        load_grp(0)
        if n_grp4 > 1:
            load_grp(1)
        for grp in range(n_grp4):
            gpar = grp % 2
            upg, dng = upgs[gpar], dngs[gpar]
            i0 = grp * GSZ
            gi = min(GSZ, IT - i0)
            for tcc in range((PTOK + 511) // 512):
                Wc = min(512, PTOK - tcc * 512)
                T.wait_ge(sw[0].h, lwus[grp])
                V.wait_ge(spe.h, pe_dn_last)
                pe_up = {}
                for it in range(gi):
                    T.wait_ge(ssc.h, relu_done.get(it % 2, 0))
                    for c in range(DC):
                        mm = T.matmul(
                            PS[it % 2][:, 0:Wc],
                            upg[:, c * gi * 128 + it * 128: c * gi * 128 + (it + 1) * 128],
                            cs[:, c * PTOK + tcc * 512: c * PTOK + tcc * 512 + Wc],
                            start=(c == 0), stop=(c == DC - 1))
                    pe_up[it] = spe.inc(mm, 1)
                    hb = h["hrel"] if it % 2 == 0 else h["hrel2"]
                    C.wait_ge(spe.h, pe_up[it])
                    C.wait_ge(sve.h, sq_done.get(it % 2, 0))
                    C.activation(out=hb[:, 0:Wc], in_=PS[it % 2][:, 0:Wc],
                                 func=AF.Relu)
                    relu_done[it % 2] = ssc.n
                    V.wait_ge(ssc.h, relu_done[it % 2])
                    V.tensor_mul(h2slv[:, it * 512: it * 512 + Wc],
                                 hb[:, 0:Wc], hb[:, 0:Wc])
                    sq_done[it % 2] = ve_h2 = sve.n
                T.wait_ge(sve.h, ve_h2)
                T.wait_ge(sw[1].h, lwds[grp])
                add_done = {}
                dscs = [h["epa"], h["epb"]]
                for o in range(H):
                    ob = 2 + o % 4
                    T.wait_ge(ssc.h, evac_done.get(o % 4, 0))
                    for it in range(gi):
                        mm = T.matmul(PS[ob][:, 0:Wc],
                                      dng[:, it * D + o * 128: it * D + (o + 1) * 128],
                                      h2slv[:, it * 512: it * 512 + Wc],
                                      start=(it == 0), stop=(it == gi - 1))
                    pe_dn = spe.inc(mm, 1)
                    C.wait_ge(spe.h, pe_dn)
                    C.wait_ge(sve.h, add_done.get(o % 2, 0))
                    C.copy(out=dscs[o % 2][:, 0:Wc], in_=PS[ob][:, 0:Wc])
                    evac_done[o % 4] = ssc.n
                    V.wait_ge(ssc.h, evac_done[o % 4])
                    ssl = sslv[:, o * PTOK + tcc * 512: o * PTOK + tcc * 512 + Wc]
                    V.tensor_tensor(ssl, ssl, dscs[o % 2][:, 0:Wc], ALU.add)
                    add_done[o % 2] = sve.n
                pe_dn_last = spe.n
            pe_grp_done[grp] = spe.n
            if grp + 2 < n_grp4:
                load_grp(grp + 2)
        for b_ in upgs + dngs:
            A.release(b_)
        A.release(h2slv)
        A.release(cs)
        zslv2 = A.alloc(DC * 512 * 2)
        outsv = A.alloc(DC * 512 * 4, F32)
        xfbv = A.alloc(DC * 512 * 4, F32)
        Y.wait_ge(spe.h, spe.n)
        ve_ffn_done = sve.n
        for tcc in range((PTOK + 511) // 512):
            Wc = min(512, PTOK - tcc * 512)
            yoff = tcc * 512
            T.wait_ge(sve.h, ve_ffn_done)
            C.wait_ge(sve.h, ve_ffn_done)
            vstat = stats(sslv[:, yoff:], Wc, PTOK)
            V.wait_ge(sve.h, vstat)
            scc = 0
            for c in range(DC):
                V.tensor_mul(h["ktmp"][:, 0:Wc],
                             sslv[:, c * PTOK + yoff: c * PTOK + yoff + Wc],
                             h["rsbc"][:, 0:Wc])
                V.tensor_sub(h["ktmp"][:, 0:Wc], h["ktmp"][:, 0:Wc],
                                          h["msbc"][:, 0:Wc])
                vv = sve.n
                V.tensor_scalar(out=zslv2[:, c * Wc:(c + 1) * Wc],
                                             in0=h["ktmp"][:, 0:Wc],
                                             scalar1=h["gmbb"][:, c:c + 1],
                                             scalar2=h["gmbb"][:, DC + c:DC + c + 1],
                                             op0=ALU.mult, op1=ALU.add)
                vv = sve.n
            C.wait_ge(sve.h, vv)
            T.wait_ge(sve.h, vv)
            vstat = stats(zslv2, Wc, Wc)
            lx = dma(Y, sx[1], 
                out=xfbv[:, 0:DC * Wc],
                in_=dmaj(x2sp, poff + yoff, poff + yoff + Wc))
            V.wait_ge(sve.h, vstat)
            V.wait_ge(sx[1].h, lx)
            for c in range(DC):
                V.tensor_mul(h["ktmp"][:, 0:Wc], zslv2[:, c * Wc:(c + 1) * Wc],
                             h["rsbc"][:, 0:Wc])
                V.tensor_sub(h["ktmp"][:, 0:Wc], h["ktmp"][:, 0:Wc], h["msbc"][:, 0:Wc])
                V.tensor_tensor(outsv[:, c * Wc:(c + 1) * Wc],
                                             xfbv[:, c * Wc:(c + 1) * Wc],
                                             h["ktmp"][:, 0:Wc], ALU.add)
                vv = sve.n
            Y.wait_ge(sve.h, vv)
            st_out = dma(Y, sst, out=dmaj(out_t, poff + yoff, poff + yoff + Wc),
                                         in_=outsv[:, 0:DC * Wc])
            V.wait_ge(sst.h, st_out)
        A.release(sslv)
        A.release(t2sb_c)
        A.release(t2bcb_c)
        A.release(zslv2)
        A.release(outsv)
        A.release(xfbv)
    Y.wait_ge(sst.h, sst.n)



_NC_CACHE = {}


def kernel(L0_attn_x, L0_attn_Wq, L0_attn_Wk, L0_attn_Wv, L0_attn_Wo,
           L0_ffn_gate_w, L0_ffn_up_w, L0_ffn_down_w, L0_ffn_expert_weight,
           L0_ffn_ln_gamma, L0_ffn_ln_beta):
    cfg = derive(FULL_CFG)
    D, S = cfg["D"], cfg["S"]

    x = np.ascontiguousarray(np.asarray(L0_attn_x, dtype=np.float32)[0].T)
    wq_p = perm_wqk(np.asarray(L0_attn_Wq, dtype=np.float32))
    wk_p = perm_wqk(np.asarray(L0_attn_Wk, dtype=np.float32))
    wv_a = np.ascontiguousarray(np.asarray(L0_attn_Wv, dtype=np.float32))
    wo_a = np.ascontiguousarray(np.asarray(L0_attn_Wo, dtype=np.float32))
    up_a = np.ascontiguousarray(np.asarray(L0_ffn_up_w, dtype=np.float32))
    dn_a = np.ascontiguousarray(np.asarray(L0_ffn_down_w, dtype=np.float32))
    ex_a = np.ascontiguousarray(np.asarray(L0_ffn_expert_weight, dtype=np.float32))
    gt_a = np.ascontiguousarray(np.asarray(L0_ffn_gate_w, dtype=np.float32))
    gmb = np.ascontiguousarray(np.stack([
        np.asarray(L0_ffn_ln_gamma, dtype=np.float32),
        np.asarray(L0_ffn_ln_beta, dtype=np.float32)]))
    COS, SIN = host_rope_tables(cfg)
    identm = np.eye(128, dtype=np.float32)
    pswap = np.zeros((128, 128), dtype=np.float32)
    for m_ in range(64):
        pswap[64 + m_, m_] = 1.0
        pswap[m_, 64 + m_] = 1.0

    if "full" not in _NC_CACHE:
        _NC_CACHE["full"] = build_nc(FULL_CFG)
    nc = _NC_CACHE["full"]

    in_maps = []
    for c in range(8):
        in_maps.append(dict(
            xt=x, wq=wq_p, wk=wk_p, wv=wv_a, wo=wo_a,
            upw=up_a, dnw=dn_a, exw=ex_a, gtw=gt_a, gmb=gmb,
            cosk=COS, sink=SIN, ident=identm, pswap=pswap,
            maskin=np.asarray(host_masks(cfg, c)),
        ))
    res = run_bass_kernel_spmd(nc, in_maps, core_ids=list(range(8)))
    out = np.empty((S, D), dtype=np.float32)
    for c in range(8):
        t0, t1 = cfg["runs"][c]
        ntok = (t1 - t0) * 128
        out[t0 * 128: t1 * 128, :] = res.results[c]["out_t"][:, 0:ntok].T
    return out[None, :, :]


# version 13
# speedup vs baseline: 1.1771x; 1.1771x over previous
import math
import numpy as np
import concourse.bass as bass
import concourse.mybir as mybir
from concourse.bass_utils import run_bass_kernel_spmd

F32 = mybir.dt.float32
BF16 = mybir.dt.bfloat16
AF = mybir.ActivationFunctionType
ALU = mybir.AluOpType
AX = mybir.AxisListType

SQRT_HD_INV = 1.0 / math.sqrt(128.0)
EXP_BIAS = -30.0
EPS = 1e-5

FULL_CFG = dict(
    D=2048, H=16, S=8192, INTER=8192, NEXP=8,
    runs=[(0, 14), (14, 26), (26, 36), (36, 44), (44, 50), (50, 56), (56, 60), (60, 64)],
)


def derive(cfg):
    c = dict(cfg)
    c["DC"] = c["D"] // 128
    c["NTILE"] = c["S"] // 128
    c["IT"] = c["INTER"] // 128
    assert c["H"] * 128 == c["D"]
    assert sum(b - a for a, b in c["runs"]) == c["NTILE"]
    c["NTOK_MAX"] = max(b - a for a, b in c["runs"]) * 128
    return c


def plan_for(cfg, core):
    t0, t1 = cfg["runs"][core]
    p = {"t0": t0, "t1": t1, "ntok": (t1 - t0) * 128, "M": t1}
    blocks = []
    t = 0
    while t < t1:
        w = min(4, t1 - t)
        blocks.append((t * 128, w * 128))
        t += w
    p["kvblocks"] = blocks
    qcs = []
    t = t0
    while t < t1:
        w = min(4, t1 - t)
        qcs.append((t, w))
        t += w
    p["qchunks"] = qcs
    passes = []
    npass = (p["ntok"] + 1023) // 1024
    base = p["ntok"] // npass // 128
    rem = p["ntok"] // 128 - base * npass
    off = 0
    for i in range(npass):
        w = (base + (1 if i < rem else 0)) * 128
        passes.append((off, w))
        off += w
    p["ffn_passes"] = passes
    moff = 0
    mask_off = []
    for (ft, wt) in qcs:
        mask_off.append(moff)
        moff += wt * (wt * 128)
    p["mask_off"] = mask_off
    p["mask_cols"] = max(moff, 128)
    return p


def host_masks(cfg, core):
    import ml_dtypes
    p = plan_for(cfg, core)
    maskm = max(plan_for(cfg, c)["mask_cols"] for c in range(8))
    m = np.zeros((128, maskm), dtype=np.float32)
    for (ft, wt), moff in zip(p["qchunks"], p["mask_off"]):
        W = wt * 128
        q_tok0 = ft * 128
        for dk in range(wt):
            kc = ft + dk
            blk = (kc * 128 + np.arange(128)[:, None]) <= (q_tok0 + np.arange(W)[None, :])
            m[:, moff + dk * W: moff + (dk + 1) * W] = blk.astype(np.float32)
    return m.astype(ml_dtypes.bfloat16)


def host_rope_tables(cfg):
    S = cfg["S"]
    freqs = 1.0 / 10000.0 ** (np.arange(0, 128, 2, dtype=np.float32) / 128.0)
    pos = np.arange(S, dtype=np.float32)
    ang = np.outer(freqs.astype(np.float32), pos)
    cosv = np.cos(ang).astype(np.float32)
    sinv = np.sin(ang).astype(np.float32)
    COS = np.concatenate([cosv, cosv], axis=0)
    SIN = np.concatenate([-sinv, sinv], axis=0)
    return np.ascontiguousarray(COS), np.ascontiguousarray(SIN)


def perm_wqk(w):
    D = w.shape[1]
    idx = np.arange(D).reshape(D // 128, 128)
    perm = np.concatenate([idx[:, 0::2], idx[:, 1::2]], axis=1).reshape(-1)
    return np.ascontiguousarray(w[:, perm])


class Ctr:
    def __init__(self, h):
        self.h = h
        self.n = 0

    def inc(self, inst, k):
        inst.then_inc(self.h, k)
        self.n += k
        return self.n


class SerialEng:
    _PASS = {"wait_ge", "partition_id", "dma_start", "Switch", "nop", "engine"}

    def __init__(self, eng, ctr):
        self._e = eng
        self._c = ctr

    def __getattr__(self, name):
        fn = getattr(self._e, name)
        if name in SerialEng._PASS or not callable(fn):
            return fn

        def wrapped(*a, **k):
            self._e.wait_ge(self._c.h, self._c.n)
            inst = fn(*a, **k)
            self._c.n += 1
            inst.then_inc(self._c.h, 1)
            return inst
        return wrapped


def dmaj(ap2d, a, b):
    return ap2d[:, a:b].rearrange("(c p) t -> p c t", p=128)


class Arena:
    def __init__(self, tensor, nbytes):
        self.t = tensor
        self.free = [(0, nbytes)]
        self.toks = {}

    def alloc(self, nbytes, dtype=BF16):
        n = (int(nbytes) + 511) & ~511
        for i, (o, s) in enumerate(self.free):
            if s >= n:
                if s == n:
                    self.free.pop(i)
                else:
                    self.free[i] = (o + n, s - n)
                view = self.t[:, o // 2:(o + n) // 2]
                if dtype is F32:
                    view = view.bitcast(F32)
                self.toks[id(view)] = (o, n)
                return view
        raise MemoryError(f"arena alloc {n}B failed; free={self.free}")

    def release(self, view):
        o, n = self.toks.pop(id(view))
        self.free.append((o, n))
        self.free.sort()
        merged = []
        for seg in self.free:
            if merged and merged[-1][0] + merged[-1][1] == seg[0]:
                merged[-1] = (merged[-1][0], merged[-1][1] + seg[1])
            else:
                merged.append(list(seg) if False else seg)
        self.free = [tuple(x) for x in merged]


def dma(eng, ctr, out, in_, **kw):
    eng.wait_ge(ctr.h, ctr.n)
    return ctr.inc(eng.dma_start(out=out, in_=in_, **kw), 16)


def build_nc(cfg):
    cfg = derive(cfg)
    D, DC, H, S = cfg["D"], cfg["DC"], cfg["H"], cfg["S"]
    INTER, IT, NEXP = cfg["INTER"], cfg["IT"], cfg["NEXP"]
    NTILE, NTOKM = cfg["NTILE"], cfg["NTOK_MAX"]
    MASKM = max(plan_for(cfg, c)["mask_cols"] for c in range(8))

    nc = bass.Bass("TRN2", target_bir_lowering=False, num_devices=8)

    h = {}

    def din(name, shape, dt):
        h[name] = nc.dram_tensor(name, shape, dt, kind="ExternalInput").ap()

    din("xt", [D, S], F32)
    din("wq", [D, D], F32)
    din("wk", [D, D], F32)
    din("wv", [D, D], F32)
    din("wo", [D, D], F32)
    din("upw", [D, INTER], F32)
    din("dnw", [INTER, D], F32)
    din("exw", [D, D], F32)
    din("gtw", [D, NEXP], F32)
    din("gmb", [2, D], F32)
    din("cosk", [128, S], F32)
    din("sink", [128, S], F32)
    din("ident", [128, 128], F32)
    din("pswap", [128, 128], F32)
    din("maskin", [128, MASKM], BF16)
    h["out_t"] = nc.dram_tensor("out_t", [D, NTOKM], F32, kind="ExternalOutput").ap()

    h["kT"] = nc.dram_tensor("kT", [D, S], BF16, kind="Internal").ap()
    h["vhb"] = nc.dram_tensor("vhb", [H, NTILE, 128, 128], BF16, kind="Internal").ap()
    h["xhsp"] = nc.dram_tensor("xhsp", [D, S], BF16, kind="Internal").ap()
    h["x2sp"] = nc.dram_tensor("x2sp", [D, NTOKM], F32, kind="Internal").ap()
    h["xh2sp"] = nc.dram_tensor("xh2sp", [D, NTOKM], BF16, kind="Internal").ap()
    h["t2sp"] = nc.dram_tensor("t2sp", [8, 128], F32, kind="Internal").ap()

    sb = nc.alloc_sbuf_tensor
    h["sqsl"] = sb("sqsl", [128, DC * 512], BF16)
    h["epa"] = sb("epa", [128, 1024], BF16)
    h["epb"] = sb("epb", [128, 1024], BF16)
    h["ktmp"] = sb("ktmp", [128, 512], BF16)
    h["ktmp2"] = sb("ktmp2", [128, 512], BF16)
    h["xtm"] = sb("xtm", [128, 512], BF16)
    h["rtmp"] = sb("rtmp", [128, 512], BF16)
    h["hrel"] = sb("hrel", [128, 512], BF16)
    h["hrel2"] = sb("hrel2", [128, 512], BF16)
    h["cosb"] = sb("cosb", [128, 1024], BF16)
    h["sinb"] = sb("sinb", [128, 1024], BF16)
    h["wstr"] = sb("wstr", [128, 2 * DC * 256], BF16)
    h["gateb"] = sb("gateb", [128, DC * NEXP], BF16)
    h["linv"] = sb("linv", [128, 512], F32)
    h["lgs"] = sb("lgs", [128, 64], F32)
    h["t2col"] = sb("t2col", [128, 8], F32)
    h["t2row"] = sb("t2row", [8, 128], F32)
    h["identb"] = sb("identb", [128, 128], F32)
    h["pswapb"] = sb("pswapb", [128, 128], BF16)
    h["gmbb"] = sb("gmbb", [128, 2 * DC], F32)
    h["st_mu"] = sb("st_mu", [1, 512], F32)
    h["st_rs"] = sb("st_rs", [1, 512], F32)
    h["st_ms"] = sb("st_ms", [1, 512], F32)
    h["st_t1"] = sb("st_t1", [1, 512], F32)
    h["msbc"] = sb("msbc", [128, 512], BF16)
    h["rsbc"] = sb("rsbc", [128, 512], BF16)
    h["ones_c"] = sb("ones_c", [128, 1], BF16)
    h["ones_m"] = sb("ones_m", [128, 128], BF16)
    h["ones_r"] = sb("ones_r", [1, 128], F32)
    h["biasm30"] = sb("biasm30", [128, 1], F32)
    h["epsr"] = sb("epsr", [1, 1], F32)
    rem = nc.sbuf_bytes_remaining
    arena_bytes = (rem - 2048) & ~511
    h["ARENA_BYTES"] = arena_bytes
    h["arena"] = sb("arena", [128, arena_bytes // 2], BF16)

    h["PSALL"] = nc.alloc_psum_tensor("psall", [128, 4096], F32)
    h["PS"] = [h["PSALL"][:, i * 512:(i + 1) * 512] for i in range(8)]

    import contextlib
    ctx = contextlib.ExitStack()
    sem_names = ["sx0", "sx1", "sk0", "sk1", "sm", "sw0", "sw1", "swm",
                 "sg0", "sg1", "sst", "spe", "ssc", "sve"]
    sems = {n: Ctr(ctx.enter_context(nc.semaphore(n))) for n in sem_names}

    T, G, Y = nc.tensor, nc.gpsimd, nc.sync
    V = SerialEng(nc.vector, sems["sve"])
    C = SerialEng(nc.scalar, sems["ssc"])

    V.memset(h["ones_c"][:, :], 1.0)
    V.memset(h["ones_m"][:, :], 1.0)
    V.memset(h["ones_r"][:, :], 1.0)
    V.memset(h["biasm30"][:, :], EXP_BIAS)
    V.memset(h["epsr"][:, :], EPS)
    ve_pre = sems["sve"].n
    dma(Y, sems["sm"], 
        out=h["gmbb"][:, 0:DC],
        in_=h["gmb"][0:1, :].rearrange("o (c p) -> p (o c)", p=128),
        allow_slow_non_contiguous=True)
    dma(Y, sems["sm"], 
        out=h["gmbb"][:, DC:2 * DC],
        in_=h["gmb"][1:2, :].rearrange("o (c p) -> p (o c)", p=128),
        allow_slow_non_contiguous=True)
    dma(Y, sems["sm"], out=h["identb"][:, :], in_=h["ident"][:, :])
    dma(G, sems["swm"], out=h["pswapb"][:, :], in_=h["pswap"][:, :])
    sm_pre = sems["sm"].n
    h["ve_pre"], h["sm_pre"] = ve_pre, sm_pre

    engines = [T, nc.vector, nc.scalar, G, Y]
    pre_state = {n: s.n for n, s in sems.items()}
    for core in nc.Switch(engines=engines,
                          index=[e.partition_id() for e in engines], n=8):
        for n, s in sems.items():
            s.n = pre_state[n]
        emit_body(nc, cfg, plan_for(cfg, core), sems, h)

    ctx.close()
    return nc


def emit_body(nc, cfg, p, sems, h):
    D, DC, H, S = cfg["D"], cfg["DC"], cfg["H"], cfg["S"]
    INTER, IT, NEXP = cfg["INTER"], cfg["IT"], cfg["NEXP"]
    T, G, Y = nc.tensor, nc.gpsimd, nc.sync
    V = SerialEng(nc.vector, sems["sve"])
    C = SerialEng(nc.scalar, sems["ssc"])
    PS = h["PS"]
    sx = [sems["sx0"], sems["sx1"]]
    sg = [sems["sg0"], sems["sg1"]]
    sk = [sems["sk0"], sems["sk1"]]
    sw = [sems["sw0"], sems["sw1"]]
    sm, swm, sst = sems["sm"], sems["swm"], sems["sst"]
    spe, ssc, sve = sems["spe"], sems["ssc"], sems["sve"]

    xt, kT, vhb, xhsp = h["xt"], h["kT"], h["vhb"], h["xhsp"]
    x2sp, xh2sp, t2sp, out_t = h["x2sp"], h["xh2sp"], h["t2sp"], h["out_t"]

    def stats(src_ap, TW, stride):
        if stride == TW:
            nh = (DC * TW + 4095) // 4096
            for k in range(nh):
                lo = k * 4096
                hi = min(DC * TW, lo + 4096)
                C.activation(out=h["sqsl"][:, lo:hi],
                             in_=src_ap[:, lo:hi], func=AF.Square)
        else:
            for c in range(DC):
                C.activation(out=h["sqsl"][:, c * TW:(c + 1) * TW],
                             in_=src_ap[:, c * stride: c * stride + TW],
                             func=AF.Square)
        scc = ssc.n
        for c in range(DC):
            mm = T.matmul(PS[6][0:1, 0:TW], h["ones_c"][:, 0:1],
                          src_ap[:, c * stride: c * stride + TW],
                          start=(c == 0), stop=(c == DC - 1))
        spe.inc(mm, 1)
        T.wait_ge(ssc.h, scc)
        for c in range(DC):
            mm = T.matmul(PS[7][0:1, 0:TW], h["ones_c"][:, 0:1],
                          h["sqsl"][:, c * TW:(c + 1) * TW],
                          start=(c == 0), stop=(c == DC - 1))
        last_sq = spe.inc(mm, 1)
        V.wait_ge(spe.h, last_sq)
        V.tensor_scalar_mul(h["st_mu"][:, 0:TW], PS[6][0:1, 0:TW], 1.0 / D)
        V.tensor_mul(h["st_ms"][:, 0:TW], h["st_mu"][:, 0:TW], h["st_mu"][:, 0:TW])
        V.scalar_tensor_tensor(h["st_t1"][:, 0:TW], PS[7][0:1, 0:TW], 1.0 / D,
                               h["st_ms"][:, 0:TW], ALU.mult, ALU.subtract)
        vv = sve.n
        C.wait_ge(sve.h, vv)
        C.activation(out=h["st_t1"][:, 0:TW], in_=h["st_t1"][:, 0:TW],
                     func=AF.Ln, bias=h["epsr"][:, :])
        C.activation(out=h["st_rs"][:, 0:TW], in_=h["st_t1"][:, 0:TW],
                     func=AF.Exp, scale=-0.5)
        cc = ssc.n
        V.wait_ge(ssc.h, cc)
        V.tensor_mul(h["st_ms"][:, 0:TW], h["st_mu"][:, 0:TW], h["st_rs"][:, 0:TW])
        vv = sve.n
        T.wait_ge(sve.h, vv)
        T.matmul(PS[6][:, 0:TW], h["ones_r"][:, :], h["st_rs"][0:1, 0:TW],
                 start=True, stop=True)
        pe = spe.inc(T.matmul(PS[7][:, 0:TW], h["ones_r"][:, :],
                              h["st_ms"][0:1, 0:TW], start=True, stop=True), 1)
        V.wait_ge(spe.h, pe)
        V.tensor_copy(h["rsbc"][:, 0:TW], PS[6][:, 0:TW])
        V.tensor_copy(h["msbc"][:, 0:TW], PS[7][:, 0:TW])
        done = sve.n
        T.wait_ge(sve.h, done)
        return done

    for e in (T, V, C, G, Y):
        e.wait_ge(sems["sve"].h, h["ve_pre"])
        e.wait_ge(sems["sm"].h, h["sm_pre"])
        e.wait_ge(sems["swm"].h, 16)

    A = Arena(h["arena"], h["ARENA_BYTES"])
    KBW = 512
    wbig = A.alloc(DC * D * 2)
    xbfs = [A.alloc(DC * KBW * 2), A.alloc(DC * KBW * 2)]
    kvsl = A.alloc(DC * KBW * 2)
    xhscs = [A.alloc(DC * KBW * 2), A.alloc(DC * KBW * 2)]
    nblk = len(p["kvblocks"])

    w0 = dma(G, sw[0], 
        out=wbig[:, 0:DC * D].rearrange("p (c o) -> p c o", o=D),
        in_=h["wk"].rearrange("(c p) o -> p c o", p=128))
    ve_xdone = {}
    st_kv = {}
    st_xh = {}
    pe_k_done = {}
    stat_mm_of = {}
    ve_xh_of = {}
    lw_cs = {}
    kevac = {}
    ve_rope = 0
    ve_rope_b = {}
    KT = [h["ktmp"], h["ktmp2"]]
    nblk = len(p["kvblocks"])

    def kv_emit_A1(b):
        toff, BW = p["kvblocks"][b]
        buf = xbfs[b % 2]
        if b >= 2:
            G.wait_ge(spe.h, stat_mm_of[b - 2])
            G.wait_ge(sve.h, ve_xh_of[b - 2])
        lx = dma(G, sg[b % 2], out=buf[:, 0:DC * BW],
                 in_=dmaj(xt, toff, toff + BW))
        cpar = (b % 2) * 512
        if b >= 2:
            G.wait_ge(sve.h, ve_rope_of.get(b - 2, 0))
        lw_cs[b] = dma(G, swm, out=h["cosb"][:, cpar:cpar + BW],
                       in_=h["cosk"][:, toff:toff + BW])
        lw_cs[b] = dma(G, swm, out=h["sinb"][:, cpar:cpar + BW],
                       in_=h["sink"][:, toff:toff + BW])
        ve_xdone[b] = lx

    def kv_emit_A2a(b):
        toff, BW = p["kvblocks"][b]
        xbf = xbfs[b % 2]
        C.wait_ge(sg[b % 2].h, ve_xdone[b])
        T.wait_ge(sg[b % 2].h, ve_xdone[b])
        nh = (DC * BW + 4095) // 4096
        for k in range(nh):
            lo = k * 4096
            hi = min(DC * BW, lo + 4096)
            C.activation(out=h["sqsl"][:, lo:hi], in_=xbf[:, lo:hi],
                         func=AF.Square)
        scc = ssc.n
        for c in range(DC):
            mm = T.matmul(PS[6][0:1, 0:BW], h["ones_c"][:, 0:1],
                          xbf[:, c * BW:(c + 1) * BW],
                          start=(c == 0), stop=(c == DC - 1))
        spe.inc(mm, 1)
        T.wait_ge(ssc.h, scc)
        for c in range(DC):
            mm = T.matmul(PS[7][0:1, 0:BW], h["ones_c"][:, 0:1],
                          h["sqsl"][:, c * BW:(c + 1) * BW],
                          start=(c == 0), stop=(c == DC - 1))
        last_sq = spe.inc(mm, 1)
        stat_mm_of[b] = spe.n
        V.wait_ge(spe.h, last_sq)
        V.tensor_scalar_mul(h["st_mu"][:, 0:BW], PS[6][0:1, 0:BW], 1.0 / D)
        V.tensor_mul(h["st_ms"][:, 0:BW], h["st_mu"][:, 0:BW], h["st_mu"][:, 0:BW])
        V.scalar_tensor_tensor(h["st_t1"][:, 0:BW], PS[7][0:1, 0:BW], 1.0 / D,
                               h["st_ms"][:, 0:BW], ALU.mult, ALU.subtract)
        vv = sve.n
        C.wait_ge(sve.h, vv)
        C.activation(out=h["st_t1"][:, 0:BW], in_=h["st_t1"][:, 0:BW],
                     func=AF.Ln, bias=h["epsr"][:, :])
        C.activation(out=h["st_rs"][:, 0:BW], in_=h["st_t1"][:, 0:BW],
                     func=AF.Exp, scale=-0.5)
        cc = ssc.n
        V.wait_ge(ssc.h, cc)
        V.tensor_mul(h["st_ms"][:, 0:BW], h["st_mu"][:, 0:BW], h["st_rs"][:, 0:BW])
        stat_math_of[b] = sve.n

    def kv_emit_A2b(b):
        toff, BW = p["kvblocks"][b]
        xbf = xbfs[b % 2]
        T.wait_ge(sve.h, stat_math_of[b])
        T.matmul(PS[6][:, 0:BW], h["ones_r"][:, :], h["st_rs"][0:1, 0:BW],
                 start=True, stop=True)
        pe = spe.inc(T.matmul(PS[7][:, 0:BW], h["ones_r"][:, :],
                              h["st_ms"][0:1, 0:BW], start=True, stop=True), 1)
        V.wait_ge(spe.h, pe)
        V.tensor_copy(h["rsbc"][:, 0:BW], PS[6][:, 0:BW])
        V.tensor_copy(h["msbc"][:, 0:BW], PS[7][:, 0:BW])
        vstat = sve.n
        T.wait_ge(sve.h, vstat)
        if b >= 2:
            V.wait_ge(spe.h, pe_k_done[b - 2])
            V.wait_ge(sst.h, st_xh[b - 2])
        xh = xhscs[b % 2]
        for c in range(DC):
            V.tensor_mul(h["xtm"][:, 0:BW], xbf[:, c * BW:(c + 1) * BW],
                         h["rsbc"][:, 0:BW])
            V.tensor_sub(xh[:, c * BW:(c + 1) * BW],
                         h["xtm"][:, 0:BW], h["msbc"][:, 0:BW])
        ve_xh_of[b] = sve.n
        Y.wait_ge(sve.h, ve_xh_of[b])
        st_xh[b] = dma(Y, sst, out=dmaj(xhsp, toff, toff + BW),
                       in_=xh[:, 0:DC * BW])

    def kv_emit_B(b):
        nonlocal ve_rope
        toff, BW = p["kvblocks"][b]
        cpar = (b % 2) * 512
        xh = xhscs[b % 2]
        evac_of, pe_swap_of = {}, {}
        T.wait_ge(sve.h, ve_xh_of[b])
        if b == 0:
            T.wait_ge(sw[0].h, w0)
            V.wait_ge(sw[0].h, w0)

        def kv_swap_rope(o):
            nonlocal ve_rope
            swb = PS[2 + o % 2]
            T.wait_ge(ssc.h, evac_of[o])
            if ve_rope_b.get(o % 2):
                T.wait_ge(sve.h, ve_rope_b[o % 2])
            pe_swap_of[o] = spe.inc(
                T.matmul(swb[:, 0:BW], h["pswapb"][:, :],
                         KT[o % 2][:, 0:BW], start=True, stop=True), 1)
            V.wait_ge(spe.h, pe_swap_of[o])
            V.wait_ge(swm.h, lw_cs[b])
            if o == 0 and b >= 1:
                V.wait_ge(sst.h, st_kv[b - 1])
            ksl = kvsl[:, o * BW:(o + 1) * BW]
            V.tensor_mul(ksl, KT[o % 2][:, 0:BW], h["cosb"][:, cpar:cpar + BW])
            V.tensor_mul(h["rtmp"][:, 0:BW], swb[:, 0:BW],
                         h["sinb"][:, cpar:cpar + BW])
            V.tensor_add(ksl, ksl, h["rtmp"][:, 0:BW])
            ve_rope = sve.n
            ve_rope_b[o % 2] = ve_rope

        for o in range(H):
            T.wait_ge(ssc.h, kevac.get(o % 2, 0))
            for c in range(DC):
                mm = T.matmul(PS[o % 2][:, 0:BW],
                              wbig[:, c * D + o * 128: c * D + (o + 1) * 128],
                              xh[:, c * BW:(c + 1) * BW],
                              start=(c == 0), stop=(c == DC - 1))
            pe = spe.inc(mm, 1)
            C.wait_ge(spe.h, pe)
            if o >= 2:
                C.wait_ge(spe.h, pe_swap_of[o - 2])
                C.wait_ge(sve.h, ve_rope_b.get(o % 2, 0))
            C.copy(out=KT[o % 2][:, 0:BW], in_=PS[o % 2][:, 0:BW])
            evac_of[o] = ssc.n
            kevac[o % 2] = evac_of[o]
            if o >= 1:
                kv_swap_rope(o - 1)
        kv_swap_rope(H - 1)
        ve_rope_of[b] = ve_rope
        pe_k_done[b] = spe.n
        Y.wait_ge(sve.h, ve_rope)
        st_kv[b] = dma(Y, sst, out=dmaj(kT, toff, toff + BW),
                       in_=kvsl[:, 0:DC * BW])

    ve_rope_of = {}
    stat_math_of = {}
    for b in range(nblk):
        kv_emit_A1(b)
        kv_emit_A2a(b)
        if b >= 1:
            kv_emit_B(b - 1)
        kv_emit_A2b(b)
    kv_emit_B(nblk - 1)

    G.wait_ge(spe.h, spe.n)
    Y.wait_ge(spe.h, spe.n)
    w0 = dma(G, sw[0], 
        out=wbig[:, 0:DC * D].rearrange("p (c o) -> p c o", o=D),
        in_=h["wv"].rearrange("(c p) o -> p c o", p=128))
    xhbufs = [xbfs[0], xbfs[1]]
    st_v_last = sst.n
    pe_v_done = {}
    vevac_b = {}
    st_v_2ago = sst.n
    for b, (toff, BW) in enumerate(p["kvblocks"]):
        buf = xhbufs[b % 2]
        if b >= 2:
            Y.wait_ge(spe.h, pe_v_done[b - 2])
        Y.wait_ge(sst.h, st_xh[b])
        lx = dma(Y, sx[b % 2], out=buf[:, 0:DC * BW],
                 in_=dmaj(xhsp, toff, toff + BW))
        T.wait_ge(sx[b % 2].h, lx)
        if b == 0:
            T.wait_ge(sw[0].h, w0)
        ntt = BW // 128
        noc = (D + 511) // 512
        half_cols = kvsl.shape[1] // 2
        for tt in range(ntt):
            gt = (toff // 128) + tt
            par2 = gt % 2
            vsl = kvsl[:, par2 * half_cols: par2 * half_cols + D] \
                if half_cols >= D else kvsl[:, 0:D]
            bank0 = par2 * 4 if noc <= 4 else 0
            T.wait_ge(ssc.h, vevac_b.get(par2, 0))
            for c in range(DC):
                for oc in range(noc):
                    OW = min(512, D - oc * 512)
                    mm = T.matmul(PS[bank0 + oc][:, 0:OW],
                                  buf[:, c * BW + tt * 128: c * BW + (tt + 1) * 128],
                                  wbig[:, c * D + oc * 512: c * D + oc * 512 + OW],
                                  start=(c == 0), stop=(c == DC - 1))
            pe = spe.inc(mm, 1)
            C.wait_ge(spe.h, pe)
            C.wait_ge(sst.h, st_v_2ago if half_cols >= D else st_v_last)
            st_v_2ago = st_v_last
            for oc in range(noc):
                OW = min(512, D - oc * 512)
                C.copy(out=vsl[:, oc * 512: oc * 512 + OW],
                       in_=PS[bank0 + oc][:, 0:OW])
                scc = ssc.n
            vevac_b[par2] = scc
            Y.wait_ge(ssc.h, scc)
            st_v_last = dma(Y, sst, 
                out=vhb[:, gt:gt + 1, :, :].rearrange("a o p c -> p (o a) c"),
                in_=vsl[:, 0:D].rearrange("p (a c) -> p a c", c=128))
        pe_v_done[b] = spe.n

    A.release(kvsl)
    A.release(wbig)
    A.release(xbfs[0])
    A.release(xbfs[1])
    A.release(xhscs[0])
    A.release(xhscs[1])
    ntok, t0 = p["ntok"], p["t0"]
    qslab = A.alloc(DC * ntok * 2)
    xhoq = A.alloc(DC * ntok * 2)
    cosq_v = A.alloc(ntok * 2)
    sinq_v = A.alloc(ntok * 2)
    Y.wait_ge(spe.h, spe.n)
    Y.wait_ge(sst.h, sst.n)
    G.wait_ge(sst.h, sst.n)
    G.wait_ge(spe.h, spe.n)
    G.wait_ge(sve.h, sve.n)
    lx = dma(Y, sm, out=xhoq[:, 0:DC * ntok],
                            in_=dmaj(xhsp, t0 * 128, t0 * 128 + ntok))
    lw = dma(G, swm, out=cosq_v[:, 0:ntok],
                             in_=h["cosk"][:, t0 * 128: t0 * 128 + ntok])
    lw = dma(G, swm, out=sinq_v[:, 0:ntok],
                             in_=h["sink"][:, t0 * 128: t0 * 128 + ntok])
    T.wait_ge(sm.h, lx)
    V.wait_ge(swm.h, lw)
    nqc = (ntok + 511) // 512
    pe_q = {}
    qevac = {}
    KT = [h["ktmp"], h["ktmp2"]]
    q_evac_of, q_swap_of, q_rope_b = {}, {}, {}
    lw_qo = {}

    def load_wq(op):
        half = op % 2
        if op >= 2:
            G.wait_ge(spe.h, pe_q[2 * op - 4 + 1])
        lw_qo[op] = dma(G, sw[half],
            out=h["wstr"][:, half * DC * 256:(half + 1) * DC * 256]
            .rearrange("p (c o) -> p c o", o=256),
            in_=h["wq"][:, op * 256:(op + 1) * 256]
            .rearrange("(c p) o -> p c o", p=128))

    def q_swap_rope(idx):
        o, qc = divmod(idx, nqc)
        W = min(512, ntok - qc * 512)
        kb = idx % 2
        swb = PS[2 + kb]
        T.wait_ge(ssc.h, q_evac_of[idx])
        if q_rope_b.get(kb):
            T.wait_ge(sve.h, q_rope_b[kb])
        q_swap_of[idx] = spe.inc(
            T.matmul(swb[:, 0:W], h["pswapb"][:, :],
                     KT[kb][:, 0:W], start=True, stop=True), 1)
        V.wait_ge(spe.h, q_swap_of[idx])
        qsl = qslab[:, o * ntok + qc * 512: o * ntok + qc * 512 + W]
        cq = cosq_v[:, qc * 512: qc * 512 + W]
        sq_ = sinq_v[:, qc * 512: qc * 512 + W]
        V.tensor_mul(qsl, KT[kb][:, 0:W], cq)
        V.tensor_mul(h["rtmp"][:, 0:W], swb[:, 0:W], sq_)
        V.tensor_add(qsl, qsl, h["rtmp"][:, 0:W])
        q_rope_b[kb] = sve.n

    load_wq(0)
    if H // 2 > 1:
        load_wq(1)
    for o in range(H):
        op_ = o // 2
        half = op_ % 2
        wqv = h["wstr"][:, half * DC * 256:(half + 1) * DC * 256]
        woff = (o % 2) * 128
        T.wait_ge(sw[half].h, lw_qo[op_])
        for qc in range(nqc):
            W = min(512, ntok - qc * 512)
            idx = o * nqc + qc
            qb = idx % 2
            T.wait_ge(ssc.h, qevac.get(qb, 0))
            for c in range(DC):
                mm = T.matmul(PS[qb][:, 0:W],
                              wqv[:, c * 256 + woff: c * 256 + woff + 128],
                              xhoq[:, c * ntok + qc * 512: c * ntok + qc * 512 + W],
                              start=(c == 0), stop=(c == DC - 1))
            pe_q[o] = spe.inc(mm, 1)
            C.wait_ge(spe.h, pe_q[o])
            if idx >= 2:
                C.wait_ge(spe.h, q_swap_of[idx - 2])
                C.wait_ge(sve.h, q_rope_b.get(qb, 0))
            C.copy(out=KT[qb][:, 0:W], in_=PS[qb][:, 0:W])
            q_evac_of[idx] = ssc.n
            qevac[qb] = q_evac_of[idx]
            if idx >= 1:
                q_swap_rope(idx - 1)
        if o % 2 == 1 and o // 2 + 2 < H // 2:
            load_wq(o // 2 + 2)
    q_swap_rope(H * nqc - 1)
    ve_qrope = sve.n

    A.release(xhoq)
    A.release(cosq_v)
    A.release(sinq_v)
    M = p["M"]
    ext = M * 128
    ctx = A.alloc(DC * ntok * 2)
    khbufs = [A.alloc(ext * 2), A.alloc(ext * 2)]
    vhbufs = [A.alloc(ext * 2), A.alloc(ext * 2)]
    max_mask = max(wt_ * wt_ * 128 for (_, wt_) in p["qchunks"])
    maskv = A.alloc(max(max_mask * 2, 512))
    ve_mask = 0
    Y.wait_ge(spe.h, spe.n)
    Y.wait_ge(sve.h, sve.n)
    Y.wait_ge(ssc.h, ssc.n)
    T.wait_ge(sve.h, ve_qrope)
    Y.wait_ge(sst.h, sst.n)
    pe_head_done = {}
    PSALL = h["PSALL"]
    SApair = [PSALL[:, 0:1024], PSALL[:, 2048:3072]]
    EPAIR = [h["epa"], h["epb"]]
    LB = [PS[2], PS[6]]
    CB = [PS[3], PS[7]]
    sc_exp = {0: 0, 1: 0}
    pe_cons = {0: 0, 1: 0}
    ve_ctx_b = {0: 0, 1: 0}
    qcidx = 0
    ve_mask = 0
    for hh in range(H):
        par = hh % 2
        if hh >= 2:
            Y.wait_ge(spe.h, pe_head_done[hh - 2])
        lk = dma(Y, sk[par], out=khbufs[par][:, 0:ext],
                 in_=kT[hh * 128:(hh + 1) * 128, 0:ext])
        lk = dma(Y, sk[par], 
            out=vhbufs[par][:, 0:ext].rearrange("p (m c) -> p m c", c=128),
            in_=vhb[hh:hh + 1, 0:M, :, :].rearrange("o m p c -> p (o m) c"))
        T.wait_ge(sk[par].h, lk)
        for (qi, (ft, wt)) in enumerate(p["qchunks"]):
            W = wt * 128
            qoff = (ft - p["t0"]) * 128
            E = ft + wt
            moff = p["mask_off"][qi]
            mcols = wt * W
            if ve_mask:
                Y.wait_ge(sve.h, ve_mask)
            lmsk = dma(Y, sm, out=maskv[:, 0:mcols],
                       in_=h["maskin"][:, moff:moff + mcols])
            gates = {}

            def emit_pair(kp):
                nonlocal ve_mask
                pr = (kp // 2) % 2
                sa = SApair[pr]
                ebuf = EPAIR[pr]
                n2 = 2 if kp + 1 < E else 1
                T.wait_ge(ssc.h, sc_exp[pr])
                for j in range(n2):
                    mm = T.matmul(sa[:, j * 512: j * 512 + W],
                                  khbufs[par][:, (kp + j) * 128:(kp + j + 1) * 128],
                                  qslab[:, hh * ntok + qoff: hh * ntok + qoff + W],
                                  start=True, stop=True)
                pe_s = spe.inc(mm, 1)
                C.wait_ge(spe.h, pe_s)
                C.wait_ge(spe.h, pe_cons[pr])
                if n2 == 2 and W == 512:
                    C.activation(out=ebuf[:, 0:1024], in_=sa[:, 0:1024],
                                 func=AF.Exp, scale=SQRT_HD_INV,
                                 bias=h["biasm30"][:, :])
                else:
                    for j in range(n2):
                        C.activation(out=ebuf[:, j * 512: j * 512 + W],
                                     in_=sa[:, j * 512: j * 512 + W],
                                     func=AF.Exp, scale=SQRT_HD_INV,
                                     bias=h["biasm30"][:, :])
                e = ssc.n
                sc_exp[pr] = e
                for j in range(n2):
                    kc = kp + j
                    dk = kc - ft
                    if 0 <= dk < wt:
                        V.wait_ge(ssc.h, e)
                        V.wait_ge(sm.h, lmsk)
                        V.tensor_mul(ebuf[:, j * 512: j * 512 + W],
                                     ebuf[:, j * 512: j * 512 + W],
                                     maskv[:, dk * W: (dk + 1) * W])
                        ve_mask = sve.n
                        gates[kc] = ("ve", ve_mask)
                    else:
                        gates[kc] = ("sc", e)

            def emit_sumav_pair(kp):
                pr = (kp // 2) % 2
                ebuf = EPAIR[pr]
                n2 = 2 if kp + 1 < E else 1
                for j in range(n2):
                    kc = kp + j
                    kind, val = gates.pop(kc)
                    if kind == "ve":
                        T.wait_ge(sve.h, val)
                    else:
                        T.wait_ge(ssc.h, val)
                    if kc == 0:
                        T.wait_ge(sve.h, ve_ctx_b[qcidx % 2])
                    T.matmul(LB[qcidx % 2][:, 0:W], h["ones_m"][:, :],
                             ebuf[:, j * 512: j * 512 + W],
                             start=(kc == 0), stop=(kc == E - 1))
                    mm = T.matmul(CB[qcidx % 2][:, 0:W],
                                  vhbufs[par][:, kc * 128:(kc + 1) * 128],
                                  ebuf[:, j * 512: j * 512 + W],
                                  start=(kc == 0), stop=(kc == E - 1))
                pe_cons[pr] = spe.inc(mm, 1)

            pairs = list(range(0, E, 2))
            for pi, kp in enumerate(pairs):
                emit_pair(kp)
                if pi >= 1:
                    emit_sumav_pair(pairs[pi - 1])
            emit_sumav_pair(pairs[-1])
            pe_fin = spe.n
            V.wait_ge(spe.h, pe_fin)
            V.reciprocal(h["linv"][:, 0:W], LB[qcidx % 2][:, 0:W])
            csl = ctx[:, hh * ntok + qoff: hh * ntok + qoff + W]
            V.tensor_mul(csl, CB[qcidx % 2][:, 0:W], h["linv"][:, 0:W])
            ve_ctx_b[qcidx % 2] = sve.n
            qcidx += 1
        pe_head_done[hh] = spe.n

    A.release(qslab)
    for bb_ in khbufs + vhbufs:
        A.release(bb_)
    A.release(maskv)
    xfa2 = A.alloc(DC * 512 * 4, F32)
    xbf2 = A.alloc(DC * 512 * 2)
    xbf2b = A.alloc(DC * 512 * 2)
    Y.wait_ge(spe.h, spe.n)
    Y.wait_ge(sve.h, sve.n)
    Y.wait_ge(ssc.h, ssc.n)
    T.wait_ge(sve.h, sve.n)
    ntc = (ntok + 511) // 512
    pe_wo = spe.n
    woevac = {}
    b_evac_of = {}
    a_done = {}
    st_spill_of = {}
    st_wo_prev = sst.n
    lx_of = {}
    xbf2s = [xbf2, xbf2b]

    lx_of[0] = dma(Y, sx[0], out=xfa2[:, 0:DC * min(512, ntok)],
                   in_=dmaj(xt, t0 * 128, t0 * 128 + min(512, ntok)))

    def wo_emit_B(tc):
        nonlocal pe_wo
        TW = min(512, ntok - tc * 512)
        toff = tc * 512
        xb2 = xbf2s[tc % 2]
        lw_of, pe_wo_of = {}, {}

        def load_wo(op):
            half = op % 2
            if op >= 2:
                G.wait_ge(spe.h, pe_wo_of[2 * op - 3])
            elif pe_wo:
                G.wait_ge(spe.h, pe_wo)
            lw_of[op] = dma(G, sw[half],
                out=h["wstr"][:, half * DC * 256:(half + 1) * DC * 256]
                .rearrange("p (c o) -> p c o", o=256),
                in_=h["wo"][:, op * 256:(op + 1) * 256]
                .rearrange("(c p) o -> p c o", p=128))

        load_wo(0)
        if H // 2 > 1:
            load_wo(1)
        for o in range(H):
            op_ = o // 2
            half = op_ % 2
            wov = h["wstr"][:, half * DC * 256:(half + 1) * DC * 256]
            woff = (o % 2) * 128
            T.wait_ge(sw[half].h, lw_of[op_])
            T.wait_ge(ssc.h, woevac.get(o % 2, 0))
            for c in range(DC):
                mm = T.matmul(PS[o % 2][:, 0:TW],
                              wov[:, c * 256 + woff: c * 256 + woff + 128],
                              ctx[:, c * ntok + toff: c * ntok + toff + TW],
                              start=(c == 0), stop=(c == DC - 1))
            pe_wo = spe.inc(mm, 1)
            pe_wo_of[o] = pe_wo
            C.wait_ge(spe.h, pe_wo)
            if tc >= 2 and o == 0:
                C.wait_ge(sve.h, a_done[tc - 2])
                C.wait_ge(sst.h, st_spill_of[tc - 2])
            C.copy(out=xb2[:, o * TW:(o + 1) * TW],
                   in_=PS[o % 2][:, 0:TW])
            woevac[o % 2] = ssc.n
            if o % 2 == 1 and o // 2 + 2 < H // 2:
                load_wo(o // 2 + 2)
        b_evac_of[tc] = ssc.n

    def wo_emit_A(tc):
        nonlocal st_wo_prev
        TW = min(512, ntok - tc * 512)
        toff = tc * 512
        xb2 = xbf2s[tc % 2]
        T.wait_ge(ssc.h, b_evac_of[tc])
        vstat = stats(xb2, TW, TW)
        V.wait_ge(sve.h, vstat)
        V.wait_ge(sx[0].h, lx_of[tc])
        V.wait_ge(sst.h, st_wo_prev)
        for c in range(DC):
            V.tensor_mul(h["ktmp"][:, 0:TW], xb2[:, c * TW:(c + 1) * TW],
                         h["rsbc"][:, 0:TW])
            V.tensor_sub(h["ktmp"][:, 0:TW], h["ktmp"][:, 0:TW], h["msbc"][:, 0:TW])
            V.tensor_tensor(xfa2[:, c * TW:(c + 1) * TW],
                            xfa2[:, c * TW:(c + 1) * TW],
                            h["ktmp"][:, 0:TW], ALU.add)
        ve_x2 = sve.n
        Y.wait_ge(sve.h, ve_x2)
        dma(Y, sst, out=dmaj(x2sp, toff, toff + TW), in_=xfa2[:, 0:DC * TW])
        V.tensor_copy(xb2[:, 0:DC * TW], xfa2[:, 0:DC * TW])
        vv = sve.n
        C.wait_ge(sve.h, vv)
        T.wait_ge(sve.h, vv)
        vstat = stats(xb2, TW, TW)
        V.wait_ge(sve.h, vstat)
        for c in range(DC):
            V.tensor_mul(h["ktmp"][:, 0:TW], xb2[:, c * TW:(c + 1) * TW],
                         h["rsbc"][:, 0:TW])
            V.tensor_sub(xb2[:, c * TW:(c + 1) * TW],
                         h["ktmp"][:, 0:TW], h["msbc"][:, 0:TW])
        a_done[tc] = sve.n
        Y.wait_ge(sve.h, a_done[tc])
        dma(Y, sst, out=dmaj(xh2sp, toff, toff + TW), in_=xb2[:, 0:DC * TW])
        st_wo_prev = sst.n
        st_spill_of[tc] = sst.n
        if tc + 1 < ntc:
            TW2 = min(512, ntok - (tc + 1) * 512)
            Y.wait_ge(sst.h, st_wo_prev)
            lx_of[tc + 1] = dma(Y, sx[0],
                out=xfa2[:, 0:DC * TW2],
                in_=dmaj(xt, t0 * 128 + (tc + 1) * 512,
                         t0 * 128 + (tc + 1) * 512 + TW2))

    for tc in range(ntc):
        wo_emit_B(tc)
        if tc >= 1:
            wo_emit_A(tc - 1)
    wo_emit_A(ntc - 1)

    A.release(xfa2)
    A.release(xbf2)
    A.release(xbf2b)
    A.release(ctx)
    lwg = dma(G, swm, 
        out=h["gateb"][:, 0:DC * NEXP].rearrange("p (c e) -> p c e", e=NEXP),
        in_=h["gtw"].rearrange("(c p) e -> p c e", p=128))
    n_grp = (IT + 7) // 8
    for (poff, PTOK) in p["ffn_passes"]:
        cs = A.alloc(DC * PTOK * 2)
        sslv = A.alloc(DC * PTOK * 2)
        t2sb_c = A.alloc(4096, F32)
        t2bcb_c = A.alloc(2048)
        Y.wait_ge(sst.h, sst.n)
        Y.wait_ge(spe.h, spe.n)
        Y.wait_ge(sve.h, sve.n)
        Y.wait_ge(ssc.h, ssc.n)
        lx = dma(Y, sm, out=cs[:, 0:DC * PTOK],
                 in_=dmaj(xh2sp, poff, poff + PTOK))
        T.wait_ge(sm.h, lx)
        V.wait_ge(sm.h, lx)
        T.wait_ge(swm.h, lwg)
        V.memset(h["t2col"][:, :], 0.0)
        ntt = PTOK // 128
        for tt in range(ntt):
            for c in range(DC):
                mm = T.matmul(PS[4][:, 0:NEXP],
                              cs[:, c * PTOK + tt * 128: c * PTOK + (tt + 1) * 128],
                              h["gateb"][:, c * NEXP:(c + 1) * NEXP],
                              start=(c == 0), stop=(c == DC - 1))
            pe = spe.inc(mm, 1)
            V.wait_ge(spe.h, pe)
            V.tensor_copy(h["lgs"][:, 0:NEXP], PS[4][:, 0:NEXP])
            off = NEXP
            for k in range(1, NEXP):
                V.tensor_add(h["lgs"][:, off:off + NEXP - k],
                             h["lgs"][:, 0:NEXP - k], h["lgs"][:, k:NEXP])
                off += NEXP - k
            V.reduce_max(h["t2col"][:, tt:tt + 1],
                                      h["lgs"][:, NEXP:off], axis=AX.X)
            vv = sve.n
            T.wait_ge(sve.h, vv)
        mmt = T.transpose(PS[5][0:8, 0:128], h["t2col"][:, 0:8], h["identb"][:, :])
        pe = spe.inc(mmt, 1)
        V.wait_ge(spe.h, pe)
        V.tensor_copy(h["t2row"][:, :], PS[5][0:8, 0:128])
        vv = sve.n
        Y.wait_ge(sve.h, vv)
        ls = dma(Y, sst, out=t2sp, in_=h["t2row"][:, :])
        Y.wait_ge(sst.h, ls)
        lx2 = dma(Y, sm, out=t2sb_c[0:1, 0:PTOK],
                                 in_=t2sp.rearrange("a b -> (a b)")[0:PTOK])
        T.wait_ge(sm.h, lx2)
        for tcc in range((PTOK + 511) // 512):
            Wc = min(512, PTOK - tcc * 512)
            mm = T.matmul(PS[4][:, 0:Wc], h["ones_r"][:, :],
                          t2sb_c[0:1, tcc * 512: tcc * 512 + Wc],
                          start=True, stop=True)
            pe = spe.inc(mm, 1)
            V.wait_ge(spe.h, pe)
            V.tensor_copy(t2bcb_c[:, tcc * 512: tcc * 512 + Wc],
                                       PS[4][:, 0:Wc])
            vv = sve.n
            T.wait_ge(sve.h, vv)
        pe_moe = spe.n
        lw_mo, pe_mo_of = {}, {}

        def load_moe(op):
            half = op % 2
            if op >= 2:
                G.wait_ge(spe.h, pe_mo_of[2 * op - 3])
            else:
                G.wait_ge(spe.h, pe_moe)
            lw_mo[op] = dma(G, sw[half],
                out=h["wstr"][:, half * DC * 256:(half + 1) * DC * 256]
                .rearrange("p (c o) -> p c o", o=256),
                in_=h["exw"][:, op * 256:(op + 1) * 256]
                .rearrange("(c p) o -> p c o", p=128))

        load_moe(0)
        if H // 2 > 1:
            load_moe(1)
        for o in range(H):
            op_ = o // 2
            half = op_ % 2
            exv = h["wstr"][:, half * DC * 256:(half + 1) * DC * 256]
            eoff = (o % 2) * 128
            T.wait_ge(sw[half].h, lw_mo[op_])
            for tcc in range((PTOK + 511) // 512):
                Wc = min(512, PTOK - tcc * 512)
                for c in range(DC):
                    mm = T.matmul(PS[(o + tcc) % 2][:, 0:Wc],
                                  exv[:, c * 256 + eoff: c * 256 + eoff + 128],
                                  cs[:, c * PTOK + tcc * 512: c * PTOK + tcc * 512 + Wc],
                                  start=(c == 0), stop=(c == DC - 1))
                pe_moe = spe.inc(mm, 1)
                pe_mo_of[o] = pe_moe
                V.wait_ge(spe.h, pe_moe)
                V.tensor_mul(
                    sslv[:, o * PTOK + tcc * 512: o * PTOK + tcc * 512 + Wc],
                    PS[(o + tcc) % 2][:, 0:Wc],
                    t2bcb_c[:, tcc * 512: tcc * 512 + Wc])
                vv = sve.n
                T.wait_ge(sve.h, vv)
            if o % 2 == 1 and o // 2 + 2 < H // 2:
                load_moe(o // 2 + 2)
        GSZ = min(4, IT)
        n_grp4 = (IT + GSZ - 1) // GSZ
        upgs = [A.alloc(DC * GSZ * 128 * 2), A.alloc(DC * GSZ * 128 * 2)]
        dngs = [A.alloc(GSZ * D * 2), A.alloc(GSZ * D * 2)]
        h2slv = A.alloc(GSZ * 512 * 2)
        G.wait_ge(spe.h, spe.n)
        G.wait_ge(sve.h, sve.n)
        pe_grp_done = {}
        lwus, lwds = {}, {}
        relu_done = {}
        sq_done = {}
        evac_done = {}
        pe_dn_last = spe.n
        ve_h2 = 0

        def load_grp(g):
            gpar = g % 2
            i0_ = g * GSZ
            gi_ = min(GSZ, IT - i0_)
            if g >= 2:
                G.wait_ge(spe.h, pe_grp_done[g - 2])
            lwus[g] = dma(G, sw[0],
                out=upgs[gpar][:, 0:DC * gi_ * 128]
                .rearrange("p (c i) -> p c i", i=gi_ * 128),
                in_=h["upw"][:, i0_ * 128:(i0_ + gi_) * 128]
                .rearrange("(c p) i -> p c i", p=128))
            lwds[g] = dma(G, sw[1],
                out=dngs[gpar][:, 0:gi_ * D].rearrange("p (a o) -> p a o", o=D),
                in_=h["dnw"][i0_ * 128:(i0_ + gi_) * 128, :]
                .rearrange("(a p) o -> p a o", p=128))

        load_grp(0)
        if n_grp4 > 1:
            load_grp(1)
        for grp in range(n_grp4):
            gpar = grp % 2
            upg, dng = upgs[gpar], dngs[gpar]
            i0 = grp * GSZ
            gi = min(GSZ, IT - i0)
            for tcc in range((PTOK + 511) // 512):
                Wc = min(512, PTOK - tcc * 512)
                T.wait_ge(sw[0].h, lwus[grp])
                V.wait_ge(spe.h, pe_dn_last)
                pe_up = {}
                for it in range(gi):
                    T.wait_ge(ssc.h, relu_done.get(it % 2, 0))
                    for c in range(DC):
                        mm = T.matmul(
                            PS[it % 2][:, 0:Wc],
                            upg[:, c * gi * 128 + it * 128: c * gi * 128 + (it + 1) * 128],
                            cs[:, c * PTOK + tcc * 512: c * PTOK + tcc * 512 + Wc],
                            start=(c == 0), stop=(c == DC - 1))
                    pe_up[it] = spe.inc(mm, 1)
                    hb = h["hrel"] if it % 2 == 0 else h["hrel2"]
                    C.wait_ge(spe.h, pe_up[it])
                    C.wait_ge(sve.h, sq_done.get(it % 2, 0))
                    C.activation(out=hb[:, 0:Wc], in_=PS[it % 2][:, 0:Wc],
                                 func=AF.Relu)
                    relu_done[it % 2] = ssc.n
                    V.wait_ge(ssc.h, relu_done[it % 2])
                    V.tensor_mul(h2slv[:, it * 512: it * 512 + Wc],
                                 hb[:, 0:Wc], hb[:, 0:Wc])
                    sq_done[it % 2] = ve_h2 = sve.n
                T.wait_ge(sve.h, ve_h2)
                T.wait_ge(sw[1].h, lwds[grp])
                add_done = {}
                dscs = [h["epa"], h["epb"]]
                for o in range(H):
                    ob = 2 + o % 4
                    T.wait_ge(ssc.h, evac_done.get(o % 4, 0))
                    for it in range(gi):
                        mm = T.matmul(PS[ob][:, 0:Wc],
                                      dng[:, it * D + o * 128: it * D + (o + 1) * 128],
                                      h2slv[:, it * 512: it * 512 + Wc],
                                      start=(it == 0), stop=(it == gi - 1))
                    pe_dn = spe.inc(mm, 1)
                    C.wait_ge(spe.h, pe_dn)
                    C.wait_ge(sve.h, add_done.get(o % 2, 0))
                    C.copy(out=dscs[o % 2][:, 0:Wc], in_=PS[ob][:, 0:Wc])
                    evac_done[o % 4] = ssc.n
                    V.wait_ge(ssc.h, evac_done[o % 4])
                    ssl = sslv[:, o * PTOK + tcc * 512: o * PTOK + tcc * 512 + Wc]
                    V.tensor_tensor(ssl, ssl, dscs[o % 2][:, 0:Wc], ALU.add)
                    add_done[o % 2] = sve.n
                pe_dn_last = spe.n
            pe_grp_done[grp] = spe.n
            if grp + 2 < n_grp4:
                load_grp(grp + 2)
        for b_ in upgs + dngs:
            A.release(b_)
        A.release(h2slv)
        A.release(cs)
        zslv2 = A.alloc(DC * 512 * 2)
        outsv = A.alloc(DC * 512 * 4, F32)
        xfbv = A.alloc(DC * 512 * 4, F32)
        Y.wait_ge(spe.h, spe.n)
        ve_ffn_done = sve.n
        for tcc in range((PTOK + 511) // 512):
            Wc = min(512, PTOK - tcc * 512)
            yoff = tcc * 512
            T.wait_ge(sve.h, ve_ffn_done)
            C.wait_ge(sve.h, ve_ffn_done)
            vstat = stats(sslv[:, yoff:], Wc, PTOK)
            V.wait_ge(sve.h, vstat)
            scc = 0
            for c in range(DC):
                V.tensor_mul(h["ktmp"][:, 0:Wc],
                             sslv[:, c * PTOK + yoff: c * PTOK + yoff + Wc],
                             h["rsbc"][:, 0:Wc])
                V.tensor_sub(h["ktmp"][:, 0:Wc], h["ktmp"][:, 0:Wc],
                                          h["msbc"][:, 0:Wc])
                vv = sve.n
                V.tensor_scalar(out=zslv2[:, c * Wc:(c + 1) * Wc],
                                             in0=h["ktmp"][:, 0:Wc],
                                             scalar1=h["gmbb"][:, c:c + 1],
                                             scalar2=h["gmbb"][:, DC + c:DC + c + 1],
                                             op0=ALU.mult, op1=ALU.add)
                vv = sve.n
            C.wait_ge(sve.h, vv)
            T.wait_ge(sve.h, vv)
            vstat = stats(zslv2, Wc, Wc)
            lx = dma(Y, sx[1], 
                out=xfbv[:, 0:DC * Wc],
                in_=dmaj(x2sp, poff + yoff, poff + yoff + Wc))
            V.wait_ge(sve.h, vstat)
            V.wait_ge(sx[1].h, lx)
            for c in range(DC):
                V.tensor_mul(h["ktmp"][:, 0:Wc], zslv2[:, c * Wc:(c + 1) * Wc],
                             h["rsbc"][:, 0:Wc])
                V.tensor_sub(h["ktmp"][:, 0:Wc], h["ktmp"][:, 0:Wc], h["msbc"][:, 0:Wc])
                V.tensor_tensor(outsv[:, c * Wc:(c + 1) * Wc],
                                             xfbv[:, c * Wc:(c + 1) * Wc],
                                             h["ktmp"][:, 0:Wc], ALU.add)
                vv = sve.n
            Y.wait_ge(sve.h, vv)
            st_out = dma(Y, sst, out=dmaj(out_t, poff + yoff, poff + yoff + Wc),
                                         in_=outsv[:, 0:DC * Wc])
            V.wait_ge(sst.h, st_out)
        A.release(sslv)
        A.release(t2sb_c)
        A.release(t2bcb_c)
        A.release(zslv2)
        A.release(outsv)
        A.release(xfbv)
    Y.wait_ge(sst.h, sst.n)



_NC_CACHE = {}


def kernel(L0_attn_x, L0_attn_Wq, L0_attn_Wk, L0_attn_Wv, L0_attn_Wo,
           L0_ffn_gate_w, L0_ffn_up_w, L0_ffn_down_w, L0_ffn_expert_weight,
           L0_ffn_ln_gamma, L0_ffn_ln_beta):
    cfg = derive(FULL_CFG)
    D, S = cfg["D"], cfg["S"]

    x = np.ascontiguousarray(np.asarray(L0_attn_x, dtype=np.float32)[0].T)
    wq_p = perm_wqk(np.asarray(L0_attn_Wq, dtype=np.float32))
    wk_p = perm_wqk(np.asarray(L0_attn_Wk, dtype=np.float32))
    wv_a = np.ascontiguousarray(np.asarray(L0_attn_Wv, dtype=np.float32))
    wo_a = np.ascontiguousarray(np.asarray(L0_attn_Wo, dtype=np.float32))
    up_a = np.ascontiguousarray(np.asarray(L0_ffn_up_w, dtype=np.float32))
    dn_a = np.ascontiguousarray(np.asarray(L0_ffn_down_w, dtype=np.float32))
    ex_a = np.ascontiguousarray(np.asarray(L0_ffn_expert_weight, dtype=np.float32))
    gt_a = np.ascontiguousarray(np.asarray(L0_ffn_gate_w, dtype=np.float32))
    gmb = np.ascontiguousarray(np.stack([
        np.asarray(L0_ffn_ln_gamma, dtype=np.float32),
        np.asarray(L0_ffn_ln_beta, dtype=np.float32)]))
    COS, SIN = host_rope_tables(cfg)
    identm = np.eye(128, dtype=np.float32)
    pswap = np.zeros((128, 128), dtype=np.float32)
    for m_ in range(64):
        pswap[64 + m_, m_] = 1.0
        pswap[m_, 64 + m_] = 1.0

    if "full" not in _NC_CACHE:
        _NC_CACHE["full"] = build_nc(FULL_CFG)
    nc = _NC_CACHE["full"]

    in_maps = []
    for c in range(8):
        in_maps.append(dict(
            xt=x, wq=wq_p, wk=wk_p, wv=wv_a, wo=wo_a,
            upw=up_a, dnw=dn_a, exw=ex_a, gtw=gt_a, gmb=gmb,
            cosk=COS, sink=SIN, ident=identm, pswap=pswap,
            maskin=np.asarray(host_masks(cfg, c)),
        ))
    res = run_bass_kernel_spmd(nc, in_maps, core_ids=list(range(8)))
    out = np.empty((S, D), dtype=np.float32)
    for c in range(8):
        t0, t1 = cfg["runs"][c]
        ntok = (t1 - t0) * 128
        out[t0 * 128: t1 * 128, :] = res.results[c]["out_t"][:, 0:ntok].T
    return out[None, :, :]


# version 14
# speedup vs baseline: 1.1862x; 1.0077x over previous
import math
import numpy as np
import concourse.bass as bass
import concourse.mybir as mybir
from concourse.bass_utils import run_bass_kernel_spmd

F32 = mybir.dt.float32
BF16 = mybir.dt.bfloat16
AF = mybir.ActivationFunctionType
ALU = mybir.AluOpType
AX = mybir.AxisListType

SQRT_HD_INV = 1.0 / math.sqrt(128.0)
EXP_BIAS = -30.0
EPS = 1e-5

FULL_CFG = dict(
    D=2048, H=16, S=8192, INTER=8192, NEXP=8,
    runs=[(0, 14), (14, 26), (26, 36), (36, 44), (44, 50), (50, 56), (56, 60), (60, 64)],
)


def derive(cfg):
    c = dict(cfg)
    c["DC"] = c["D"] // 128
    c["NTILE"] = c["S"] // 128
    c["IT"] = c["INTER"] // 128
    assert c["H"] * 128 == c["D"]
    assert sum(b - a for a, b in c["runs"]) == c["NTILE"]
    c["NTOK_MAX"] = max(b - a for a, b in c["runs"]) * 128
    return c


def plan_for(cfg, core):
    t0, t1 = cfg["runs"][core]
    p = {"t0": t0, "t1": t1, "ntok": (t1 - t0) * 128, "M": t1}
    blocks = []
    t = 0
    while t < t1:
        w = min(4, t1 - t)
        blocks.append((t * 128, w * 128))
        t += w
    p["kvblocks"] = blocks
    qcs = []
    t = t0
    while t < t1:
        w = min(4, t1 - t)
        qcs.append((t, w))
        t += w
    p["qchunks"] = qcs
    passes = []
    npass = (p["ntok"] + 1023) // 1024
    base = p["ntok"] // npass // 128
    rem = p["ntok"] // 128 - base * npass
    off = 0
    for i in range(npass):
        w = (base + (1 if i < rem else 0)) * 128
        passes.append((off, w))
        off += w
    p["ffn_passes"] = passes
    moff = 0
    mask_off = []
    for (ft, wt) in qcs:
        mask_off.append(moff)
        moff += wt * (wt * 128)
    p["mask_off"] = mask_off
    p["mask_cols"] = max(moff, 128)
    return p


def host_masks(cfg, core):
    import ml_dtypes
    p = plan_for(cfg, core)
    maskm = max(plan_for(cfg, c)["mask_cols"] for c in range(8))
    m = np.zeros((128, maskm), dtype=np.float32)
    for (ft, wt), moff in zip(p["qchunks"], p["mask_off"]):
        W = wt * 128
        q_tok0 = ft * 128
        for dk in range(wt):
            kc = ft + dk
            blk = (kc * 128 + np.arange(128)[:, None]) <= (q_tok0 + np.arange(W)[None, :])
            m[:, moff + dk * W: moff + (dk + 1) * W] = blk.astype(np.float32)
    return m.astype(ml_dtypes.bfloat16)


def host_rope_tables(cfg):
    S = cfg["S"]
    freqs = 1.0 / 10000.0 ** (np.arange(0, 128, 2, dtype=np.float32) / 128.0)
    pos = np.arange(S, dtype=np.float32)
    ang = np.outer(freqs.astype(np.float32), pos)
    cosv = np.cos(ang).astype(np.float32)
    sinv = np.sin(ang).astype(np.float32)
    COS = np.concatenate([cosv, cosv], axis=0)
    SIN = np.concatenate([-sinv, sinv], axis=0)
    return np.ascontiguousarray(COS), np.ascontiguousarray(SIN)


def perm_wqk(w):
    D = w.shape[1]
    idx = np.arange(D).reshape(D // 128, 128)
    perm = np.concatenate([idx[:, 0::2], idx[:, 1::2]], axis=1).reshape(-1)
    return np.ascontiguousarray(w[:, perm])


class Ctr:
    def __init__(self, h):
        self.h = h
        self.n = 0

    def inc(self, inst, k):
        inst.then_inc(self.h, k)
        self.n += k
        return self.n


class SerialEng:
    _PASS = {"wait_ge", "partition_id", "dma_start", "Switch", "nop", "engine"}

    def __init__(self, eng, ctr):
        self._e = eng
        self._c = ctr

    def __getattr__(self, name):
        fn = getattr(self._e, name)
        if name in SerialEng._PASS or not callable(fn):
            return fn

        def wrapped(*a, **k):
            self._e.wait_ge(self._c.h, self._c.n)
            inst = fn(*a, **k)
            self._c.n += 1
            inst.then_inc(self._c.h, 1)
            return inst
        return wrapped


def dmaj(ap2d, a, b):
    return ap2d[:, a:b].rearrange("(c p) t -> p c t", p=128)


class Arena:
    def __init__(self, tensor, nbytes):
        self.t = tensor
        self.free = [(0, nbytes)]
        self.toks = {}

    def alloc(self, nbytes, dtype=BF16):
        n = (int(nbytes) + 511) & ~511
        for i, (o, s) in enumerate(self.free):
            if s >= n:
                if s == n:
                    self.free.pop(i)
                else:
                    self.free[i] = (o + n, s - n)
                view = self.t[:, o // 2:(o + n) // 2]
                if dtype is F32:
                    view = view.bitcast(F32)
                self.toks[id(view)] = (o, n)
                return view
        raise MemoryError(f"arena alloc {n}B failed; free={self.free}")

    def release(self, view):
        o, n = self.toks.pop(id(view))
        self.free.append((o, n))
        self.free.sort()
        merged = []
        for seg in self.free:
            if merged and merged[-1][0] + merged[-1][1] == seg[0]:
                merged[-1] = (merged[-1][0], merged[-1][1] + seg[1])
            else:
                merged.append(list(seg) if False else seg)
        self.free = [tuple(x) for x in merged]


def dma(eng, ctr, out, in_, **kw):
    eng.wait_ge(ctr.h, ctr.n)
    return ctr.inc(eng.dma_start(out=out, in_=in_, **kw), 16)


def build_nc(cfg):
    cfg = derive(cfg)
    D, DC, H, S = cfg["D"], cfg["DC"], cfg["H"], cfg["S"]
    INTER, IT, NEXP = cfg["INTER"], cfg["IT"], cfg["NEXP"]
    NTILE, NTOKM = cfg["NTILE"], cfg["NTOK_MAX"]
    MASKM = max(plan_for(cfg, c)["mask_cols"] for c in range(8))

    nc = bass.Bass("TRN2", target_bir_lowering=False, num_devices=8)

    h = {}

    def din(name, shape, dt):
        h[name] = nc.dram_tensor(name, shape, dt, kind="ExternalInput").ap()

    din("xt", [D, S], F32)
    din("wq", [D, D], F32)
    din("wk", [D, D], F32)
    din("wv", [D, D], F32)
    din("wo", [D, D], F32)
    din("upw", [D, INTER], F32)
    din("dnw", [INTER, D], F32)
    din("exw", [D, D], F32)
    din("gtw", [D, NEXP], F32)
    din("gmb", [2, D], F32)
    din("cosk", [128, S], F32)
    din("sink", [128, S], F32)
    din("ident", [128, 128], F32)
    din("pswap", [128, 128], F32)
    din("maskin", [128, MASKM], BF16)
    h["out_t"] = nc.dram_tensor("out_t", [D, NTOKM], F32, kind="ExternalOutput").ap()

    h["kT"] = nc.dram_tensor("kT", [D, S], BF16, kind="Internal").ap()
    h["vhb"] = nc.dram_tensor("vhb", [H, NTILE, 128, 128], BF16, kind="Internal").ap()
    h["xhsp"] = nc.dram_tensor("xhsp", [D, S], BF16, kind="Internal").ap()
    h["x2sp"] = nc.dram_tensor("x2sp", [D, NTOKM], F32, kind="Internal").ap()
    h["xh2sp"] = nc.dram_tensor("xh2sp", [D, NTOKM], BF16, kind="Internal").ap()
    h["t2sp"] = nc.dram_tensor("t2sp", [8, 128], F32, kind="Internal").ap()

    sb = nc.alloc_sbuf_tensor
    h["sqsl"] = sb("sqsl", [128, DC * 512], BF16)
    h["epa"] = sb("epa", [128, 1024], BF16)
    h["epb"] = sb("epb", [128, 1024], BF16)
    h["ktmp"] = sb("ktmp", [128, 512], BF16)
    h["ktmp2"] = sb("ktmp2", [128, 512], BF16)
    h["xtm"] = sb("xtm", [128, 512], BF16)
    h["rtmp"] = sb("rtmp", [128, 512], BF16)
    h["hrel"] = sb("hrel", [128, 512], BF16)
    h["hrel2"] = sb("hrel2", [128, 512], BF16)
    h["cosb"] = sb("cosb", [128, 1024], BF16)
    h["sinb"] = sb("sinb", [128, 1024], BF16)
    h["wstr"] = sb("wstr", [128, 2 * DC * 256], BF16)
    h["gateb"] = sb("gateb", [128, DC * NEXP], BF16)
    h["linv"] = sb("linv", [128, 512], F32)
    h["lgs"] = sb("lgs", [128, 64], F32)
    h["t2col"] = sb("t2col", [128, 8], F32)
    h["t2row"] = sb("t2row", [8, 128], F32)
    h["identb"] = sb("identb", [128, 128], F32)
    h["pswapb"] = sb("pswapb", [128, 128], BF16)
    h["gmbb"] = sb("gmbb", [128, 2 * DC], F32)
    h["st_mu"] = sb("st_mu", [1, 512], F32)
    h["st_rs"] = sb("st_rs", [1, 512], F32)
    h["st_ms"] = sb("st_ms", [1, 512], F32)
    h["st_t1"] = sb("st_t1", [1, 512], F32)
    h["msbc"] = sb("msbc", [128, 512], BF16)
    h["rsbc"] = sb("rsbc", [128, 512], BF16)
    h["ones_c"] = sb("ones_c", [128, 1], BF16)
    h["ones_m"] = sb("ones_m", [128, 128], BF16)
    h["ones_r"] = sb("ones_r", [1, 128], F32)
    h["biasm30"] = sb("biasm30", [128, 1], F32)
    h["epsr"] = sb("epsr", [1, 1], F32)
    rem = nc.sbuf_bytes_remaining
    arena_bytes = (rem - 2048) & ~511
    h["ARENA_BYTES"] = arena_bytes
    h["arena"] = sb("arena", [128, arena_bytes // 2], BF16)

    h["PSALL"] = nc.alloc_psum_tensor("psall", [128, 4096], F32)
    h["PS"] = [h["PSALL"][:, i * 512:(i + 1) * 512] for i in range(8)]

    import contextlib
    ctx = contextlib.ExitStack()
    sem_names = ["sx0", "sx1", "sk0", "sk1", "sm", "sw0", "sw1", "swm",
                 "sg0", "sg1", "sst", "spe", "ssc", "sve"]
    sems = {n: Ctr(ctx.enter_context(nc.semaphore(n))) for n in sem_names}

    T, G, Y = nc.tensor, nc.gpsimd, nc.sync
    V = SerialEng(nc.vector, sems["sve"])
    C = SerialEng(nc.scalar, sems["ssc"])

    V.memset(h["ones_c"][:, :], 1.0)
    V.memset(h["ones_m"][:, :], 1.0)
    V.memset(h["ones_r"][:, :], 1.0)
    V.memset(h["biasm30"][:, :], EXP_BIAS)
    V.memset(h["epsr"][:, :], EPS)
    ve_pre = sems["sve"].n
    dma(Y, sems["sm"], 
        out=h["gmbb"][:, 0:DC],
        in_=h["gmb"][0:1, :].rearrange("o (c p) -> p (o c)", p=128),
        allow_slow_non_contiguous=True)
    dma(Y, sems["sm"], 
        out=h["gmbb"][:, DC:2 * DC],
        in_=h["gmb"][1:2, :].rearrange("o (c p) -> p (o c)", p=128),
        allow_slow_non_contiguous=True)
    dma(Y, sems["sm"], out=h["identb"][:, :], in_=h["ident"][:, :])
    dma(G, sems["swm"], out=h["pswapb"][:, :], in_=h["pswap"][:, :])
    sm_pre = sems["sm"].n
    h["ve_pre"], h["sm_pre"] = ve_pre, sm_pre

    engines = [T, nc.vector, nc.scalar, G, Y]
    pre_state = {n: s.n for n, s in sems.items()}
    for core in nc.Switch(engines=engines,
                          index=[e.partition_id() for e in engines], n=8):
        for n, s in sems.items():
            s.n = pre_state[n]
        emit_body(nc, cfg, plan_for(cfg, core), sems, h)

    ctx.close()
    return nc


def emit_body(nc, cfg, p, sems, h):
    D, DC, H, S = cfg["D"], cfg["DC"], cfg["H"], cfg["S"]
    INTER, IT, NEXP = cfg["INTER"], cfg["IT"], cfg["NEXP"]
    T, G, Y = nc.tensor, nc.gpsimd, nc.sync
    V = SerialEng(nc.vector, sems["sve"])
    C = SerialEng(nc.scalar, sems["ssc"])
    PS = h["PS"]
    sx = [sems["sx0"], sems["sx1"]]
    sg = [sems["sg0"], sems["sg1"]]
    sk = [sems["sk0"], sems["sk1"]]
    sw = [sems["sw0"], sems["sw1"]]
    sm, swm, sst = sems["sm"], sems["swm"], sems["sst"]
    spe, ssc, sve = sems["spe"], sems["ssc"], sems["sve"]

    xt, kT, vhb, xhsp = h["xt"], h["kT"], h["vhb"], h["xhsp"]
    x2sp, xh2sp, t2sp, out_t = h["x2sp"], h["xh2sp"], h["t2sp"], h["out_t"]

    def stats(src_ap, TW, stride):
        if stride == TW:
            nh = (DC * TW + 4095) // 4096
            for k in range(nh):
                lo = k * 4096
                hi = min(DC * TW, lo + 4096)
                C.activation(out=h["sqsl"][:, lo:hi],
                             in_=src_ap[:, lo:hi], func=AF.Square)
        else:
            for c in range(DC):
                C.activation(out=h["sqsl"][:, c * TW:(c + 1) * TW],
                             in_=src_ap[:, c * stride: c * stride + TW],
                             func=AF.Square)
        scc = ssc.n
        for c in range(DC):
            mm = T.matmul(PS[6][0:1, 0:TW], h["ones_c"][:, 0:1],
                          src_ap[:, c * stride: c * stride + TW],
                          start=(c == 0), stop=(c == DC - 1))
        spe.inc(mm, 1)
        T.wait_ge(ssc.h, scc)
        for c in range(DC):
            mm = T.matmul(PS[7][0:1, 0:TW], h["ones_c"][:, 0:1],
                          h["sqsl"][:, c * TW:(c + 1) * TW],
                          start=(c == 0), stop=(c == DC - 1))
        last_sq = spe.inc(mm, 1)
        V.wait_ge(spe.h, last_sq)
        V.tensor_scalar_mul(h["st_mu"][:, 0:TW], PS[6][0:1, 0:TW], 1.0 / D)
        V.tensor_mul(h["st_ms"][:, 0:TW], h["st_mu"][:, 0:TW], h["st_mu"][:, 0:TW])
        V.scalar_tensor_tensor(h["st_t1"][:, 0:TW], PS[7][0:1, 0:TW], 1.0 / D,
                               h["st_ms"][:, 0:TW], ALU.mult, ALU.subtract)
        vv = sve.n
        C.wait_ge(sve.h, vv)
        C.activation(out=h["st_t1"][:, 0:TW], in_=h["st_t1"][:, 0:TW],
                     func=AF.Ln, bias=h["epsr"][:, :])
        C.activation(out=h["st_rs"][:, 0:TW], in_=h["st_t1"][:, 0:TW],
                     func=AF.Exp, scale=-0.5)
        cc = ssc.n
        V.wait_ge(ssc.h, cc)
        V.tensor_mul(h["st_ms"][:, 0:TW], h["st_mu"][:, 0:TW], h["st_rs"][:, 0:TW])
        vv = sve.n
        T.wait_ge(sve.h, vv)
        T.matmul(PS[6][:, 0:TW], h["ones_r"][:, :], h["st_rs"][0:1, 0:TW],
                 start=True, stop=True)
        pe = spe.inc(T.matmul(PS[7][:, 0:TW], h["ones_r"][:, :],
                              h["st_ms"][0:1, 0:TW], start=True, stop=True), 1)
        V.wait_ge(spe.h, pe)
        V.tensor_copy(h["rsbc"][:, 0:TW], PS[6][:, 0:TW])
        V.tensor_copy(h["msbc"][:, 0:TW], PS[7][:, 0:TW])
        done = sve.n
        T.wait_ge(sve.h, done)
        return done

    for e in (T, V, C, G, Y):
        e.wait_ge(sems["sve"].h, h["ve_pre"])
        e.wait_ge(sems["sm"].h, h["sm_pre"])
        e.wait_ge(sems["swm"].h, 16)

    A = Arena(h["arena"], h["ARENA_BYTES"])
    KBW = 512
    wbig = A.alloc(DC * D * 2)
    xbfs = [A.alloc(DC * KBW * 2), A.alloc(DC * KBW * 2)]
    kvsl = A.alloc(DC * KBW * 2)
    xhscs = [A.alloc(DC * KBW * 2), A.alloc(DC * KBW * 2)]
    nblk = len(p["kvblocks"])

    w0 = dma(G, sw[0], 
        out=wbig[:, 0:DC * D].rearrange("p (c o) -> p c o", o=D),
        in_=h["wk"].rearrange("(c p) o -> p c o", p=128))
    ve_xdone = {}
    st_kv = {}
    st_xh = {}
    pe_k_done = {}
    stat_mm_of = {}
    ve_xh_of = {}
    lw_cs = {}
    kevac = {}
    ve_rope = 0
    ve_rope_b = {}
    KT = [h["ktmp"], h["ktmp2"]]
    nblk = len(p["kvblocks"])

    def kv_emit_A1(b):
        toff, BW = p["kvblocks"][b]
        buf = xbfs[b % 2]
        if b >= 2:
            G.wait_ge(spe.h, stat_mm_of[b - 2])
            G.wait_ge(sve.h, ve_xh_of[b - 2])
        lx = dma(G, sg[b % 2], out=buf[:, 0:DC * BW],
                 in_=dmaj(xt, toff, toff + BW))
        cpar = (b % 2) * 512
        if b >= 2:
            G.wait_ge(sve.h, ve_rope_of.get(b - 2, 0))
        lw_cs[b] = dma(G, swm, out=h["cosb"][:, cpar:cpar + BW],
                       in_=h["cosk"][:, toff:toff + BW])
        lw_cs[b] = dma(G, swm, out=h["sinb"][:, cpar:cpar + BW],
                       in_=h["sink"][:, toff:toff + BW])
        ve_xdone[b] = lx

    def kv_emit_A2a(b):
        toff, BW = p["kvblocks"][b]
        xbf = xbfs[b % 2]
        C.wait_ge(sg[b % 2].h, ve_xdone[b])
        T.wait_ge(sg[b % 2].h, ve_xdone[b])
        nh = (DC * BW + 4095) // 4096
        for k in range(nh):
            lo = k * 4096
            hi = min(DC * BW, lo + 4096)
            C.activation(out=h["sqsl"][:, lo:hi], in_=xbf[:, lo:hi],
                         func=AF.Square)
        scc = ssc.n
        for c in range(DC):
            mm = T.matmul(PS[6][0:1, 0:BW], h["ones_c"][:, 0:1],
                          xbf[:, c * BW:(c + 1) * BW],
                          start=(c == 0), stop=(c == DC - 1))
        spe.inc(mm, 1)
        T.wait_ge(ssc.h, scc)
        for c in range(DC):
            mm = T.matmul(PS[7][0:1, 0:BW], h["ones_c"][:, 0:1],
                          h["sqsl"][:, c * BW:(c + 1) * BW],
                          start=(c == 0), stop=(c == DC - 1))
        last_sq = spe.inc(mm, 1)
        stat_mm_of[b] = spe.n
        V.wait_ge(spe.h, last_sq)
        V.tensor_scalar_mul(h["st_mu"][:, 0:BW], PS[6][0:1, 0:BW], 1.0 / D)
        V.tensor_mul(h["st_ms"][:, 0:BW], h["st_mu"][:, 0:BW], h["st_mu"][:, 0:BW])
        V.scalar_tensor_tensor(h["st_t1"][:, 0:BW], PS[7][0:1, 0:BW], 1.0 / D,
                               h["st_ms"][:, 0:BW], ALU.mult, ALU.subtract)
        vv = sve.n
        C.wait_ge(sve.h, vv)
        C.activation(out=h["st_t1"][:, 0:BW], in_=h["st_t1"][:, 0:BW],
                     func=AF.Ln, bias=h["epsr"][:, :])
        C.activation(out=h["st_rs"][:, 0:BW], in_=h["st_t1"][:, 0:BW],
                     func=AF.Exp, scale=-0.5)
        cc = ssc.n
        V.wait_ge(ssc.h, cc)
        V.tensor_mul(h["st_ms"][:, 0:BW], h["st_mu"][:, 0:BW], h["st_rs"][:, 0:BW])
        stat_math_of[b] = sve.n

    def kv_emit_A2b(b):
        toff, BW = p["kvblocks"][b]
        xbf = xbfs[b % 2]
        T.wait_ge(sve.h, stat_math_of[b])
        T.matmul(PS[6][:, 0:BW], h["ones_r"][:, :], h["st_rs"][0:1, 0:BW],
                 start=True, stop=True)
        pe = spe.inc(T.matmul(PS[7][:, 0:BW], h["ones_r"][:, :],
                              h["st_ms"][0:1, 0:BW], start=True, stop=True), 1)
        V.wait_ge(spe.h, pe)
        V.tensor_copy(h["rsbc"][:, 0:BW], PS[6][:, 0:BW])
        V.tensor_copy(h["msbc"][:, 0:BW], PS[7][:, 0:BW])
        vstat = sve.n
        T.wait_ge(sve.h, vstat)
        if b >= 2:
            V.wait_ge(spe.h, pe_k_done[b - 2])
            V.wait_ge(sst.h, st_xh[b - 2])
        xh = xhscs[b % 2]
        for c in range(DC):
            V.tensor_mul(h["xtm"][:, 0:BW], xbf[:, c * BW:(c + 1) * BW],
                         h["rsbc"][:, 0:BW])
            V.tensor_sub(xh[:, c * BW:(c + 1) * BW],
                         h["xtm"][:, 0:BW], h["msbc"][:, 0:BW])
        ve_xh_of[b] = sve.n
        Y.wait_ge(sve.h, ve_xh_of[b])
        st_xh[b] = dma(Y, sst, out=dmaj(xhsp, toff, toff + BW),
                       in_=xh[:, 0:DC * BW])

    def kv_emit_B(b):
        nonlocal ve_rope
        toff, BW = p["kvblocks"][b]
        cpar = (b % 2) * 512
        xh = xhscs[b % 2]
        evac_of, pe_swap_of = {}, {}
        T.wait_ge(sve.h, ve_xh_of[b])
        if b == 0:
            T.wait_ge(sw[0].h, w0)
            V.wait_ge(sw[0].h, w0)

        def kv_swap_rope(o):
            nonlocal ve_rope
            swb = PS[2 + o % 2]
            T.wait_ge(ssc.h, evac_of[o])
            if ve_rope_b.get(o % 2):
                T.wait_ge(sve.h, ve_rope_b[o % 2])
            pe_swap_of[o] = spe.inc(
                T.matmul(swb[:, 0:BW], h["pswapb"][:, :],
                         KT[o % 2][:, 0:BW], start=True, stop=True), 1)
            V.wait_ge(spe.h, pe_swap_of[o])
            V.wait_ge(swm.h, lw_cs[b])
            if o == 0 and b >= 1:
                V.wait_ge(sst.h, st_kv[b - 1])
            ksl = kvsl[:, o * BW:(o + 1) * BW]
            V.tensor_mul(ksl, KT[o % 2][:, 0:BW], h["cosb"][:, cpar:cpar + BW])
            V.tensor_mul(h["rtmp"][:, 0:BW], swb[:, 0:BW],
                         h["sinb"][:, cpar:cpar + BW])
            V.tensor_add(ksl, ksl, h["rtmp"][:, 0:BW])
            ve_rope = sve.n
            ve_rope_b[o % 2] = ve_rope

        for o in range(H):
            T.wait_ge(ssc.h, kevac.get(o % 2, 0))
            for c in range(DC):
                mm = T.matmul(PS[o % 2][:, 0:BW],
                              wbig[:, c * D + o * 128: c * D + (o + 1) * 128],
                              xh[:, c * BW:(c + 1) * BW],
                              start=(c == 0), stop=(c == DC - 1))
            pe = spe.inc(mm, 1)
            C.wait_ge(spe.h, pe)
            if o >= 2:
                C.wait_ge(spe.h, pe_swap_of[o - 2])
                C.wait_ge(sve.h, ve_rope_b.get(o % 2, 0))
            C.copy(out=KT[o % 2][:, 0:BW], in_=PS[o % 2][:, 0:BW])
            evac_of[o] = ssc.n
            kevac[o % 2] = evac_of[o]
            if o >= 1:
                kv_swap_rope(o - 1)
        kv_swap_rope(H - 1)
        ve_rope_of[b] = ve_rope
        pe_k_done[b] = spe.n
        Y.wait_ge(sve.h, ve_rope)
        st_kv[b] = dma(Y, sst, out=dmaj(kT, toff, toff + BW),
                       in_=kvsl[:, 0:DC * BW])

    ve_rope_of = {}
    stat_math_of = {}
    for b in range(nblk):
        kv_emit_A1(b)
        kv_emit_A2a(b)
        if b >= 1:
            kv_emit_B(b - 1)
        kv_emit_A2b(b)
    kv_emit_B(nblk - 1)

    G.wait_ge(spe.h, spe.n)
    Y.wait_ge(spe.h, spe.n)
    NWG = min(4, DC)
    gsz_w = (DC + NWG - 1) // NWG
    w0s = []
    for g_ in range(NWG):
        c0, c1 = g_ * gsz_w, min(DC, (g_ + 1) * gsz_w)
        w0s.append(dma(G, sw[0],
            out=wbig[:, c0 * D:c1 * D].rearrange("p (c o) -> p c o", o=D),
            in_=h["wv"].rearrange("(c p) o -> p c o", p=128)[:, c0:c1, :]))
    xhbufs = [xbfs[0], xbfs[1]]
    st_v_last = sst.n
    pe_v_done = {}
    vevac_b = {}
    st_v_2ago = sst.n
    for b, (toff, BW) in enumerate(p["kvblocks"]):
        buf = xhbufs[b % 2]
        if b >= 2:
            Y.wait_ge(spe.h, pe_v_done[b - 2])
        Y.wait_ge(sst.h, st_xh[b])
        lx = dma(Y, sx[b % 2], out=buf[:, 0:DC * BW],
                 in_=dmaj(xhsp, toff, toff + BW))
        T.wait_ge(sx[b % 2].h, lx)
        ntt = BW // 128
        noc = (D + 511) // 512
        half_cols = kvsl.shape[1] // 2
        for tt in range(ntt):
            gt = (toff // 128) + tt
            par2 = gt % 2
            vsl = kvsl[:, par2 * half_cols: par2 * half_cols + D] \
                if half_cols >= D else kvsl[:, 0:D]
            bank0 = par2 * 4 if noc <= 4 else 0
            T.wait_ge(ssc.h, vevac_b.get(par2, 0))
            for c in range(DC):
                if b == 0 and tt == 0 and c % gsz_w == 0:
                    T.wait_ge(sw[0].h, w0s[c // gsz_w])
                for oc in range(noc):
                    OW = min(512, D - oc * 512)
                    mm = T.matmul(PS[bank0 + oc][:, 0:OW],
                                  buf[:, c * BW + tt * 128: c * BW + (tt + 1) * 128],
                                  wbig[:, c * D + oc * 512: c * D + oc * 512 + OW],
                                  start=(c == 0), stop=(c == DC - 1))
            pe = spe.inc(mm, 1)
            C.wait_ge(spe.h, pe)
            C.wait_ge(sst.h, st_v_2ago if half_cols >= D else st_v_last)
            st_v_2ago = st_v_last
            for oc in range(noc):
                OW = min(512, D - oc * 512)
                C.copy(out=vsl[:, oc * 512: oc * 512 + OW],
                       in_=PS[bank0 + oc][:, 0:OW])
                scc = ssc.n
            vevac_b[par2] = scc
            Y.wait_ge(ssc.h, scc)
            st_v_last = dma(Y, sst, 
                out=vhb[:, gt:gt + 1, :, :].rearrange("a o p c -> p (o a) c"),
                in_=vsl[:, 0:D].rearrange("p (a c) -> p a c", c=128))
        pe_v_done[b] = spe.n

    A.release(kvsl)
    A.release(wbig)
    A.release(xbfs[0])
    A.release(xbfs[1])
    A.release(xhscs[0])
    A.release(xhscs[1])
    ntok, t0 = p["ntok"], p["t0"]
    qslab = A.alloc(DC * ntok * 2)
    xhoq = A.alloc(DC * ntok * 2)
    cosq_v = A.alloc(ntok * 2)
    sinq_v = A.alloc(ntok * 2)
    Y.wait_ge(spe.h, spe.n)
    Y.wait_ge(sst.h, sst.n)
    G.wait_ge(sst.h, sst.n)
    G.wait_ge(spe.h, spe.n)
    G.wait_ge(sve.h, sve.n)
    lx = dma(Y, sm, out=xhoq[:, 0:DC * ntok],
                            in_=dmaj(xhsp, t0 * 128, t0 * 128 + ntok))
    lw = dma(G, swm, out=cosq_v[:, 0:ntok],
                             in_=h["cosk"][:, t0 * 128: t0 * 128 + ntok])
    lw = dma(G, swm, out=sinq_v[:, 0:ntok],
                             in_=h["sink"][:, t0 * 128: t0 * 128 + ntok])
    T.wait_ge(sm.h, lx)
    V.wait_ge(swm.h, lw)
    nqc = (ntok + 511) // 512
    pe_q = {}
    qevac = {}
    KT = [h["ktmp"], h["ktmp2"]]
    q_evac_of, q_swap_of, q_rope_b = {}, {}, {}
    lw_qo = {}

    def load_wq(op):
        half = op % 2
        if op >= 2:
            G.wait_ge(spe.h, pe_q[2 * op - 4 + 1])
        lw_qo[op] = dma(G, sw[half],
            out=h["wstr"][:, half * DC * 256:(half + 1) * DC * 256]
            .rearrange("p (c o) -> p c o", o=256),
            in_=h["wq"][:, op * 256:(op + 1) * 256]
            .rearrange("(c p) o -> p c o", p=128))

    def q_swap_rope(idx):
        o, qc = divmod(idx, nqc)
        W = min(512, ntok - qc * 512)
        kb = idx % 2
        swb = PS[2 + kb]
        T.wait_ge(ssc.h, q_evac_of[idx])
        if q_rope_b.get(kb):
            T.wait_ge(sve.h, q_rope_b[kb])
        q_swap_of[idx] = spe.inc(
            T.matmul(swb[:, 0:W], h["pswapb"][:, :],
                     KT[kb][:, 0:W], start=True, stop=True), 1)
        V.wait_ge(spe.h, q_swap_of[idx])
        qsl = qslab[:, o * ntok + qc * 512: o * ntok + qc * 512 + W]
        cq = cosq_v[:, qc * 512: qc * 512 + W]
        sq_ = sinq_v[:, qc * 512: qc * 512 + W]
        V.tensor_mul(qsl, KT[kb][:, 0:W], cq)
        V.tensor_mul(h["rtmp"][:, 0:W], swb[:, 0:W], sq_)
        V.tensor_add(qsl, qsl, h["rtmp"][:, 0:W])
        q_rope_b[kb] = sve.n

    load_wq(0)
    if H // 2 > 1:
        load_wq(1)
    for o in range(H):
        op_ = o // 2
        half = op_ % 2
        wqv = h["wstr"][:, half * DC * 256:(half + 1) * DC * 256]
        woff = (o % 2) * 128
        T.wait_ge(sw[half].h, lw_qo[op_])
        for qc in range(nqc):
            W = min(512, ntok - qc * 512)
            idx = o * nqc + qc
            qb = idx % 2
            T.wait_ge(ssc.h, qevac.get(qb, 0))
            for c in range(DC):
                mm = T.matmul(PS[qb][:, 0:W],
                              wqv[:, c * 256 + woff: c * 256 + woff + 128],
                              xhoq[:, c * ntok + qc * 512: c * ntok + qc * 512 + W],
                              start=(c == 0), stop=(c == DC - 1))
            pe_q[o] = spe.inc(mm, 1)
            C.wait_ge(spe.h, pe_q[o])
            if idx >= 2:
                C.wait_ge(spe.h, q_swap_of[idx - 2])
                C.wait_ge(sve.h, q_rope_b.get(qb, 0))
            C.copy(out=KT[qb][:, 0:W], in_=PS[qb][:, 0:W])
            q_evac_of[idx] = ssc.n
            qevac[qb] = q_evac_of[idx]
            if idx >= 1:
                q_swap_rope(idx - 1)
        if o % 2 == 1 and o // 2 + 2 < H // 2:
            load_wq(o // 2 + 2)
    q_swap_rope(H * nqc - 1)
    ve_qrope = sve.n

    A.release(xhoq)
    A.release(cosq_v)
    A.release(sinq_v)
    M = p["M"]
    ext = M * 128
    ctx = A.alloc(DC * ntok * 2)
    khbufs = [A.alloc(ext * 2), A.alloc(ext * 2)]
    vhbufs = [A.alloc(ext * 2), A.alloc(ext * 2)]
    max_mask = max(wt_ * wt_ * 128 for (_, wt_) in p["qchunks"])
    maskv = A.alloc(max(max_mask * 2, 512))
    ve_mask = 0
    Y.wait_ge(spe.h, spe.n)
    Y.wait_ge(sve.h, sve.n)
    Y.wait_ge(ssc.h, ssc.n)
    T.wait_ge(sve.h, ve_qrope)
    Y.wait_ge(sst.h, sst.n)
    pe_head_done = {}
    PSALL = h["PSALL"]
    SApair = [PSALL[:, 0:1024], PSALL[:, 2048:3072]]
    EPAIR = [h["epa"], h["epb"]]
    LB = [PS[2], PS[6]]
    CB = [PS[3], PS[7]]
    sc_exp = {0: 0, 1: 0}
    pe_cons = {0: 0, 1: 0}
    ve_ctx_b = {0: 0, 1: 0}
    qcidx = 0
    ve_mask = 0
    for hh in range(H):
        par = hh % 2
        if hh >= 2:
            Y.wait_ge(spe.h, pe_head_done[hh - 2])
        lk = dma(Y, sk[par], out=khbufs[par][:, 0:ext],
                 in_=kT[hh * 128:(hh + 1) * 128, 0:ext])
        lk = dma(Y, sk[par], 
            out=vhbufs[par][:, 0:ext].rearrange("p (m c) -> p m c", c=128),
            in_=vhb[hh:hh + 1, 0:M, :, :].rearrange("o m p c -> p (o m) c"))
        T.wait_ge(sk[par].h, lk)
        for (qi, (ft, wt)) in enumerate(p["qchunks"]):
            W = wt * 128
            qoff = (ft - p["t0"]) * 128
            E = ft + wt
            moff = p["mask_off"][qi]
            mcols = wt * W
            if ve_mask:
                Y.wait_ge(sve.h, ve_mask)
            lmsk = dma(Y, sm, out=maskv[:, 0:mcols],
                       in_=h["maskin"][:, moff:moff + mcols])
            gates = {}

            def emit_pair(kp):
                nonlocal ve_mask
                pr = (kp // 2) % 2
                sa = SApair[pr]
                ebuf = EPAIR[pr]
                n2 = 2 if kp + 1 < E else 1
                T.wait_ge(ssc.h, sc_exp[pr])
                for j in range(n2):
                    mm = T.matmul(sa[:, j * 512: j * 512 + W],
                                  khbufs[par][:, (kp + j) * 128:(kp + j + 1) * 128],
                                  qslab[:, hh * ntok + qoff: hh * ntok + qoff + W],
                                  start=True, stop=True)
                pe_s = spe.inc(mm, 1)
                C.wait_ge(spe.h, pe_s)
                C.wait_ge(spe.h, pe_cons[pr])
                if n2 == 2 and W == 512:
                    C.activation(out=ebuf[:, 0:1024], in_=sa[:, 0:1024],
                                 func=AF.Exp, scale=SQRT_HD_INV,
                                 bias=h["biasm30"][:, :])
                else:
                    for j in range(n2):
                        C.activation(out=ebuf[:, j * 512: j * 512 + W],
                                     in_=sa[:, j * 512: j * 512 + W],
                                     func=AF.Exp, scale=SQRT_HD_INV,
                                     bias=h["biasm30"][:, :])
                e = ssc.n
                sc_exp[pr] = e
                for j in range(n2):
                    kc = kp + j
                    dk = kc - ft
                    if 0 <= dk < wt:
                        V.wait_ge(ssc.h, e)
                        V.wait_ge(sm.h, lmsk)
                        V.tensor_mul(ebuf[:, j * 512: j * 512 + W],
                                     ebuf[:, j * 512: j * 512 + W],
                                     maskv[:, dk * W: (dk + 1) * W])
                        ve_mask = sve.n
                        gates[kc] = ("ve", ve_mask)
                    else:
                        gates[kc] = ("sc", e)

            def emit_sumav_pair(kp):
                pr = (kp // 2) % 2
                ebuf = EPAIR[pr]
                n2 = 2 if kp + 1 < E else 1
                for j in range(n2):
                    kc = kp + j
                    kind, val = gates.pop(kc)
                    if kind == "ve":
                        T.wait_ge(sve.h, val)
                    else:
                        T.wait_ge(ssc.h, val)
                    if kc == 0:
                        T.wait_ge(sve.h, ve_ctx_b[qcidx % 2])
                    T.matmul(LB[qcidx % 2][:, 0:W], h["ones_m"][:, :],
                             ebuf[:, j * 512: j * 512 + W],
                             start=(kc == 0), stop=(kc == E - 1))
                    mm = T.matmul(CB[qcidx % 2][:, 0:W],
                                  vhbufs[par][:, kc * 128:(kc + 1) * 128],
                                  ebuf[:, j * 512: j * 512 + W],
                                  start=(kc == 0), stop=(kc == E - 1))
                pe_cons[pr] = spe.inc(mm, 1)

            pairs = list(range(0, E, 2))
            for pi, kp in enumerate(pairs):
                emit_pair(kp)
                if pi >= 1:
                    emit_sumav_pair(pairs[pi - 1])
            emit_sumav_pair(pairs[-1])
            pe_fin = spe.n
            V.wait_ge(spe.h, pe_fin)
            V.reciprocal(h["linv"][:, 0:W], LB[qcidx % 2][:, 0:W])
            csl = ctx[:, hh * ntok + qoff: hh * ntok + qoff + W]
            V.tensor_mul(csl, CB[qcidx % 2][:, 0:W], h["linv"][:, 0:W])
            ve_ctx_b[qcidx % 2] = sve.n
            qcidx += 1
        pe_head_done[hh] = spe.n

    A.release(qslab)
    for bb_ in khbufs + vhbufs:
        A.release(bb_)
    A.release(maskv)
    xfa2 = A.alloc(DC * 512 * 4, F32)
    xbf2 = A.alloc(DC * 512 * 2)
    xbf2b = A.alloc(DC * 512 * 2)
    Y.wait_ge(spe.h, spe.n)
    Y.wait_ge(sve.h, sve.n)
    Y.wait_ge(ssc.h, ssc.n)
    T.wait_ge(sve.h, sve.n)
    ntc = (ntok + 511) // 512
    pe_wo = spe.n
    woevac = {}
    b_evac_of = {}
    a_done = {}
    st_spill_of = {}
    st_wo_prev = sst.n
    lx_of = {}
    xbf2s = [xbf2, xbf2b]

    lx_of[0] = dma(Y, sx[0], out=xfa2[:, 0:DC * min(512, ntok)],
                   in_=dmaj(xt, t0 * 128, t0 * 128 + min(512, ntok)))

    def wo_emit_B(tc):
        nonlocal pe_wo
        TW = min(512, ntok - tc * 512)
        toff = tc * 512
        xb2 = xbf2s[tc % 2]
        lw_of, pe_wo_of = {}, {}

        def load_wo(op):
            half = op % 2
            if op >= 2:
                G.wait_ge(spe.h, pe_wo_of[2 * op - 3])
            elif pe_wo:
                G.wait_ge(spe.h, pe_wo)
            lw_of[op] = dma(G, sw[half],
                out=h["wstr"][:, half * DC * 256:(half + 1) * DC * 256]
                .rearrange("p (c o) -> p c o", o=256),
                in_=h["wo"][:, op * 256:(op + 1) * 256]
                .rearrange("(c p) o -> p c o", p=128))

        load_wo(0)
        if H // 2 > 1:
            load_wo(1)
        for o in range(H):
            op_ = o // 2
            half = op_ % 2
            wov = h["wstr"][:, half * DC * 256:(half + 1) * DC * 256]
            woff = (o % 2) * 128
            T.wait_ge(sw[half].h, lw_of[op_])
            T.wait_ge(ssc.h, woevac.get(o % 2, 0))
            for c in range(DC):
                mm = T.matmul(PS[o % 2][:, 0:TW],
                              wov[:, c * 256 + woff: c * 256 + woff + 128],
                              ctx[:, c * ntok + toff: c * ntok + toff + TW],
                              start=(c == 0), stop=(c == DC - 1))
            pe_wo = spe.inc(mm, 1)
            pe_wo_of[o] = pe_wo
            C.wait_ge(spe.h, pe_wo)
            if tc >= 2 and o == 0:
                C.wait_ge(sve.h, a_done[tc - 2])
                C.wait_ge(sst.h, st_spill_of[tc - 2])
            C.copy(out=xb2[:, o * TW:(o + 1) * TW],
                   in_=PS[o % 2][:, 0:TW])
            woevac[o % 2] = ssc.n
            if o % 2 == 1 and o // 2 + 2 < H // 2:
                load_wo(o // 2 + 2)
        b_evac_of[tc] = ssc.n

    def wo_emit_A(tc):
        nonlocal st_wo_prev
        TW = min(512, ntok - tc * 512)
        toff = tc * 512
        xb2 = xbf2s[tc % 2]
        T.wait_ge(ssc.h, b_evac_of[tc])
        vstat = stats(xb2, TW, TW)
        V.wait_ge(sve.h, vstat)
        V.wait_ge(sx[0].h, lx_of[tc])
        V.wait_ge(sst.h, st_wo_prev)
        for c in range(DC):
            V.tensor_mul(h["ktmp"][:, 0:TW], xb2[:, c * TW:(c + 1) * TW],
                         h["rsbc"][:, 0:TW])
            V.tensor_sub(h["ktmp"][:, 0:TW], h["ktmp"][:, 0:TW], h["msbc"][:, 0:TW])
            V.tensor_tensor(xfa2[:, c * TW:(c + 1) * TW],
                            xfa2[:, c * TW:(c + 1) * TW],
                            h["ktmp"][:, 0:TW], ALU.add)
        ve_x2 = sve.n
        Y.wait_ge(sve.h, ve_x2)
        dma(Y, sst, out=dmaj(x2sp, toff, toff + TW), in_=xfa2[:, 0:DC * TW])
        V.tensor_copy(xb2[:, 0:DC * TW], xfa2[:, 0:DC * TW])
        vv = sve.n
        C.wait_ge(sve.h, vv)
        T.wait_ge(sve.h, vv)
        vstat = stats(xb2, TW, TW)
        V.wait_ge(sve.h, vstat)
        for c in range(DC):
            V.tensor_mul(h["ktmp"][:, 0:TW], xb2[:, c * TW:(c + 1) * TW],
                         h["rsbc"][:, 0:TW])
            V.tensor_sub(xb2[:, c * TW:(c + 1) * TW],
                         h["ktmp"][:, 0:TW], h["msbc"][:, 0:TW])
        a_done[tc] = sve.n
        Y.wait_ge(sve.h, a_done[tc])
        dma(Y, sst, out=dmaj(xh2sp, toff, toff + TW), in_=xb2[:, 0:DC * TW])
        st_wo_prev = sst.n
        st_spill_of[tc] = sst.n
        if tc + 1 < ntc:
            TW2 = min(512, ntok - (tc + 1) * 512)
            Y.wait_ge(sst.h, st_wo_prev)
            lx_of[tc + 1] = dma(Y, sx[0],
                out=xfa2[:, 0:DC * TW2],
                in_=dmaj(xt, t0 * 128 + (tc + 1) * 512,
                         t0 * 128 + (tc + 1) * 512 + TW2))

    for tc in range(ntc):
        wo_emit_B(tc)
        if tc >= 1:
            wo_emit_A(tc - 1)
    wo_emit_A(ntc - 1)

    A.release(xfa2)
    A.release(xbf2)
    A.release(xbf2b)
    A.release(ctx)
    lwg = dma(G, swm, 
        out=h["gateb"][:, 0:DC * NEXP].rearrange("p (c e) -> p c e", e=NEXP),
        in_=h["gtw"].rearrange("(c p) e -> p c e", p=128))
    n_grp = (IT + 7) // 8
    for (poff, PTOK) in p["ffn_passes"]:
        cs = A.alloc(DC * PTOK * 2)
        sslv = A.alloc(DC * PTOK * 2)
        t2sb_c = A.alloc(4096, F32)
        t2bcb_c = A.alloc(2048)
        Y.wait_ge(sst.h, sst.n)
        Y.wait_ge(spe.h, spe.n)
        Y.wait_ge(sve.h, sve.n)
        Y.wait_ge(ssc.h, ssc.n)
        lx = dma(Y, sm, out=cs[:, 0:DC * PTOK],
                 in_=dmaj(xh2sp, poff, poff + PTOK))
        T.wait_ge(sm.h, lx)
        V.wait_ge(sm.h, lx)
        T.wait_ge(swm.h, lwg)
        V.memset(h["t2col"][:, :], 0.0)
        ntt = PTOK // 128
        for tt in range(ntt):
            for c in range(DC):
                mm = T.matmul(PS[4][:, 0:NEXP],
                              cs[:, c * PTOK + tt * 128: c * PTOK + (tt + 1) * 128],
                              h["gateb"][:, c * NEXP:(c + 1) * NEXP],
                              start=(c == 0), stop=(c == DC - 1))
            pe = spe.inc(mm, 1)
            V.wait_ge(spe.h, pe)
            V.tensor_copy(h["lgs"][:, 0:NEXP], PS[4][:, 0:NEXP])
            off = NEXP
            for k in range(1, NEXP):
                V.tensor_add(h["lgs"][:, off:off + NEXP - k],
                             h["lgs"][:, 0:NEXP - k], h["lgs"][:, k:NEXP])
                off += NEXP - k
            V.reduce_max(h["t2col"][:, tt:tt + 1],
                                      h["lgs"][:, NEXP:off], axis=AX.X)
            vv = sve.n
            T.wait_ge(sve.h, vv)
        mmt = T.transpose(PS[5][0:8, 0:128], h["t2col"][:, 0:8], h["identb"][:, :])
        pe = spe.inc(mmt, 1)
        V.wait_ge(spe.h, pe)
        V.tensor_copy(h["t2row"][:, :], PS[5][0:8, 0:128])
        vv = sve.n
        Y.wait_ge(sve.h, vv)
        ls = dma(Y, sst, out=t2sp, in_=h["t2row"][:, :])
        Y.wait_ge(sst.h, ls)
        lx2 = dma(Y, sm, out=t2sb_c[0:1, 0:PTOK],
                                 in_=t2sp.rearrange("a b -> (a b)")[0:PTOK])
        T.wait_ge(sm.h, lx2)
        for tcc in range((PTOK + 511) // 512):
            Wc = min(512, PTOK - tcc * 512)
            mm = T.matmul(PS[4][:, 0:Wc], h["ones_r"][:, :],
                          t2sb_c[0:1, tcc * 512: tcc * 512 + Wc],
                          start=True, stop=True)
            pe = spe.inc(mm, 1)
            V.wait_ge(spe.h, pe)
            V.tensor_copy(t2bcb_c[:, tcc * 512: tcc * 512 + Wc],
                                       PS[4][:, 0:Wc])
            vv = sve.n
            T.wait_ge(sve.h, vv)
        pe_moe = spe.n
        lw_mo, pe_mo_of = {}, {}

        def load_moe(op):
            half = op % 2
            if op >= 2:
                G.wait_ge(spe.h, pe_mo_of[2 * op - 3])
            else:
                G.wait_ge(spe.h, pe_moe)
            lw_mo[op] = dma(G, sw[half],
                out=h["wstr"][:, half * DC * 256:(half + 1) * DC * 256]
                .rearrange("p (c o) -> p c o", o=256),
                in_=h["exw"][:, op * 256:(op + 1) * 256]
                .rearrange("(c p) o -> p c o", p=128))

        load_moe(0)
        if H // 2 > 1:
            load_moe(1)
        for o in range(H):
            op_ = o // 2
            half = op_ % 2
            exv = h["wstr"][:, half * DC * 256:(half + 1) * DC * 256]
            eoff = (o % 2) * 128
            T.wait_ge(sw[half].h, lw_mo[op_])
            for tcc in range((PTOK + 511) // 512):
                Wc = min(512, PTOK - tcc * 512)
                for c in range(DC):
                    mm = T.matmul(PS[(o + tcc) % 2][:, 0:Wc],
                                  exv[:, c * 256 + eoff: c * 256 + eoff + 128],
                                  cs[:, c * PTOK + tcc * 512: c * PTOK + tcc * 512 + Wc],
                                  start=(c == 0), stop=(c == DC - 1))
                pe_moe = spe.inc(mm, 1)
                pe_mo_of[o] = pe_moe
                V.wait_ge(spe.h, pe_moe)
                V.tensor_mul(
                    sslv[:, o * PTOK + tcc * 512: o * PTOK + tcc * 512 + Wc],
                    PS[(o + tcc) % 2][:, 0:Wc],
                    t2bcb_c[:, tcc * 512: tcc * 512 + Wc])
                vv = sve.n
                T.wait_ge(sve.h, vv)
            if o % 2 == 1 and o // 2 + 2 < H // 2:
                load_moe(o // 2 + 2)
        GSZ = min(4, IT)
        n_grp4 = (IT + GSZ - 1) // GSZ
        upgs = [A.alloc(DC * GSZ * 128 * 2), A.alloc(DC * GSZ * 128 * 2)]
        dngs = [A.alloc(GSZ * D * 2), A.alloc(GSZ * D * 2)]
        h2slv = A.alloc(GSZ * 512 * 2)
        G.wait_ge(spe.h, spe.n)
        G.wait_ge(sve.h, sve.n)
        pe_grp_done = {}
        lwus, lwds = {}, {}
        relu_done = {}
        sq_done = {}
        evac_done = {}
        pe_dn_last = spe.n
        ve_h2 = 0

        def load_grp(g):
            gpar = g % 2
            i0_ = g * GSZ
            gi_ = min(GSZ, IT - i0_)
            if g >= 2:
                G.wait_ge(spe.h, pe_grp_done[g - 2])
            lwus[g] = dma(G, sw[0],
                out=upgs[gpar][:, 0:DC * gi_ * 128]
                .rearrange("p (c i) -> p c i", i=gi_ * 128),
                in_=h["upw"][:, i0_ * 128:(i0_ + gi_) * 128]
                .rearrange("(c p) i -> p c i", p=128))
            lwds[g] = dma(G, sw[1],
                out=dngs[gpar][:, 0:gi_ * D].rearrange("p (a o) -> p a o", o=D),
                in_=h["dnw"][i0_ * 128:(i0_ + gi_) * 128, :]
                .rearrange("(a p) o -> p a o", p=128))

        load_grp(0)
        if n_grp4 > 1:
            load_grp(1)
        for grp in range(n_grp4):
            gpar = grp % 2
            upg, dng = upgs[gpar], dngs[gpar]
            i0 = grp * GSZ
            gi = min(GSZ, IT - i0)
            for tcc in range((PTOK + 511) // 512):
                Wc = min(512, PTOK - tcc * 512)
                T.wait_ge(sw[0].h, lwus[grp])
                V.wait_ge(spe.h, pe_dn_last)
                pe_up = {}
                for it in range(gi):
                    T.wait_ge(ssc.h, relu_done.get(it % 2, 0))
                    for c in range(DC):
                        mm = T.matmul(
                            PS[it % 2][:, 0:Wc],
                            upg[:, c * gi * 128 + it * 128: c * gi * 128 + (it + 1) * 128],
                            cs[:, c * PTOK + tcc * 512: c * PTOK + tcc * 512 + Wc],
                            start=(c == 0), stop=(c == DC - 1))
                    pe_up[it] = spe.inc(mm, 1)
                    hb = h["hrel"] if it % 2 == 0 else h["hrel2"]
                    C.wait_ge(spe.h, pe_up[it])
                    C.wait_ge(sve.h, sq_done.get(it % 2, 0))
                    C.activation(out=hb[:, 0:Wc], in_=PS[it % 2][:, 0:Wc],
                                 func=AF.Relu)
                    relu_done[it % 2] = ssc.n
                    V.wait_ge(ssc.h, relu_done[it % 2])
                    V.tensor_mul(h2slv[:, it * 512: it * 512 + Wc],
                                 hb[:, 0:Wc], hb[:, 0:Wc])
                    sq_done[it % 2] = ve_h2 = sve.n
                T.wait_ge(sve.h, ve_h2)
                T.wait_ge(sw[1].h, lwds[grp])
                add_done = {}
                dscs = [h["epa"], h["epb"]]
                for o in range(H):
                    ob = 2 + o % 4
                    T.wait_ge(ssc.h, evac_done.get(o % 4, 0))
                    for it in range(gi):
                        mm = T.matmul(PS[ob][:, 0:Wc],
                                      dng[:, it * D + o * 128: it * D + (o + 1) * 128],
                                      h2slv[:, it * 512: it * 512 + Wc],
                                      start=(it == 0), stop=(it == gi - 1))
                    pe_dn = spe.inc(mm, 1)
                    C.wait_ge(spe.h, pe_dn)
                    C.wait_ge(sve.h, add_done.get(o % 2, 0))
                    C.copy(out=dscs[o % 2][:, 0:Wc], in_=PS[ob][:, 0:Wc])
                    evac_done[o % 4] = ssc.n
                    V.wait_ge(ssc.h, evac_done[o % 4])
                    ssl = sslv[:, o * PTOK + tcc * 512: o * PTOK + tcc * 512 + Wc]
                    V.tensor_tensor(ssl, ssl, dscs[o % 2][:, 0:Wc], ALU.add)
                    add_done[o % 2] = sve.n
                pe_dn_last = spe.n
            pe_grp_done[grp] = spe.n
            if grp + 2 < n_grp4:
                load_grp(grp + 2)
        for b_ in upgs + dngs:
            A.release(b_)
        A.release(h2slv)
        A.release(cs)
        zslv2 = A.alloc(DC * 512 * 2)
        outsv = A.alloc(DC * 512 * 4, F32)
        xfbv = A.alloc(DC * 512 * 4, F32)
        Y.wait_ge(spe.h, spe.n)
        ve_ffn_done = sve.n
        ve_outer_prev = 0
        act_done = {}
        for tcc in range((PTOK + 511) // 512):
            Wc = min(512, PTOK - tcc * 512)
            yoff = tcc * 512
            T.wait_ge(sve.h, ve_ffn_done)
            C.wait_ge(sve.h, ve_ffn_done)
            vstat = stats(sslv[:, yoff:], Wc, PTOK)
            V.wait_ge(sve.h, vstat)
            C.wait_ge(sve.h, ve_outer_prev)
            inbufs = [h["ktmp2"], h["ktmp"]]
            for c in range(DC):
                tb = inbufs[c % 2]
                V.wait_ge(ssc.h, act_done.get(c % 2, 0))
                V.tensor_mul(tb[:, 0:Wc],
                             sslv[:, c * PTOK + yoff: c * PTOK + yoff + Wc],
                             h["rsbc"][:, 0:Wc])
                V.tensor_sub(tb[:, 0:Wc], tb[:, 0:Wc],
                                          h["msbc"][:, 0:Wc])
                vv = sve.n
                C.wait_ge(sve.h, vv)
                C.activation(out=zslv2[:, c * Wc:(c + 1) * Wc],
                             in_=tb[:, 0:Wc], func=AF.Identity,
                             scale=h["gmbb"][:, c:c + 1],
                             bias=h["gmbb"][:, DC + c:DC + c + 1])
                act_done[c % 2] = ssc.n
            scz = ssc.n
            T.wait_ge(ssc.h, scz)
            V.wait_ge(ssc.h, scz)
            vstat = stats(zslv2, Wc, Wc)
            lx = dma(Y, sx[1], 
                out=xfbv[:, 0:DC * Wc],
                in_=dmaj(x2sp, poff + yoff, poff + yoff + Wc))
            V.wait_ge(sve.h, vstat)
            V.wait_ge(sx[1].h, lx)
            for c in range(DC):
                V.tensor_mul(h["ktmp"][:, 0:Wc], zslv2[:, c * Wc:(c + 1) * Wc],
                             h["rsbc"][:, 0:Wc])
                V.tensor_sub(h["ktmp"][:, 0:Wc], h["ktmp"][:, 0:Wc], h["msbc"][:, 0:Wc])
                V.tensor_tensor(outsv[:, c * Wc:(c + 1) * Wc],
                                             xfbv[:, c * Wc:(c + 1) * Wc],
                                             h["ktmp"][:, 0:Wc], ALU.add)
                vv = sve.n
            ve_outer_prev = sve.n
            Y.wait_ge(sve.h, vv)
            st_out = dma(Y, sst, out=dmaj(out_t, poff + yoff, poff + yoff + Wc),
                                         in_=outsv[:, 0:DC * Wc])
            V.wait_ge(sst.h, st_out)
        A.release(sslv)
        A.release(t2sb_c)
        A.release(t2bcb_c)
        A.release(zslv2)
        A.release(outsv)
        A.release(xfbv)
    Y.wait_ge(sst.h, sst.n)



_NC_CACHE = {}


def kernel(L0_attn_x, L0_attn_Wq, L0_attn_Wk, L0_attn_Wv, L0_attn_Wo,
           L0_ffn_gate_w, L0_ffn_up_w, L0_ffn_down_w, L0_ffn_expert_weight,
           L0_ffn_ln_gamma, L0_ffn_ln_beta):
    cfg = derive(FULL_CFG)
    D, S = cfg["D"], cfg["S"]

    x = np.ascontiguousarray(np.asarray(L0_attn_x, dtype=np.float32)[0].T)
    wq_p = perm_wqk(np.asarray(L0_attn_Wq, dtype=np.float32))
    wk_p = perm_wqk(np.asarray(L0_attn_Wk, dtype=np.float32))
    wv_a = np.ascontiguousarray(np.asarray(L0_attn_Wv, dtype=np.float32))
    wo_a = np.ascontiguousarray(np.asarray(L0_attn_Wo, dtype=np.float32))
    up_a = np.ascontiguousarray(np.asarray(L0_ffn_up_w, dtype=np.float32))
    dn_a = np.ascontiguousarray(np.asarray(L0_ffn_down_w, dtype=np.float32))
    ex_a = np.ascontiguousarray(np.asarray(L0_ffn_expert_weight, dtype=np.float32))
    gt_a = np.ascontiguousarray(np.asarray(L0_ffn_gate_w, dtype=np.float32))
    gmb = np.ascontiguousarray(np.stack([
        np.asarray(L0_ffn_ln_gamma, dtype=np.float32),
        np.asarray(L0_ffn_ln_beta, dtype=np.float32)]))
    COS, SIN = host_rope_tables(cfg)
    identm = np.eye(128, dtype=np.float32)
    pswap = np.zeros((128, 128), dtype=np.float32)
    for m_ in range(64):
        pswap[64 + m_, m_] = 1.0
        pswap[m_, 64 + m_] = 1.0

    if "full" not in _NC_CACHE:
        _NC_CACHE["full"] = build_nc(FULL_CFG)
    nc = _NC_CACHE["full"]

    in_maps = []
    for c in range(8):
        in_maps.append(dict(
            xt=x, wq=wq_p, wk=wk_p, wv=wv_a, wo=wo_a,
            upw=up_a, dnw=dn_a, exw=ex_a, gtw=gt_a, gmb=gmb,
            cosk=COS, sink=SIN, ident=identm, pswap=pswap,
            maskin=np.asarray(host_masks(cfg, c)),
        ))
    res = run_bass_kernel_spmd(nc, in_maps, core_ids=list(range(8)))
    out = np.empty((S, D), dtype=np.float32)
    for c in range(8):
        t0, t1 = cfg["runs"][c]
        ntok = (t1 - t0) * 128
        out[t0 * 128: t1 * 128, :] = res.results[c]["out_t"][:, 0:ntok].T
    return out[None, :, :]
